# version 01
# baseline (speedup 1.0000x reference)
import numpy as np
import concourse.bass as bass
import concourse.mybir as mybir
from concourse.bass_utils import run_bass_kernel_spmd

F32 = mybir.dt.float32
AF = mybir.ActivationFunctionType
ALU = mybir.AluOpType
AX = mybir.AxisListType

B, S, D = 4, 2048, 2048
NC = 8
H, HD = 16, 128
HPC = 8
TOK = 1024
NCH = 16
NBLK = 4
EPS_LN = 1e-5
EPS_L0 = 1e-6
FMIN = float(np.finfo(np.float32).min)
SCALE_L0 = float(HD ** (-0.5))
ROPE_BASE = 10000.0

DEBUG_TAPS = False



def _split_perm():
    return np.concatenate([np.arange(0, 128, 2), np.arange(1, 128, 2)])


def _perm_cols_per_head(w, n_heads=16):
    p = _split_perm()
    w = w.reshape(w.shape[0], n_heads, 128)[:, :, p]
    return np.ascontiguousarray(w.reshape(w.shape[0], -1))


def _rope_tables():
    half = 64
    pos = np.arange(S, dtype=np.float64)
    inv = 1.0 / (ROPE_BASE ** (np.arange(half, dtype=np.float64) * 2.0 / 128.0))
    ang = pos[:, None] * inv[None, :]
    cosT = np.cos(ang).T.astype(np.float32)
    sinT = np.sin(ang).T.astype(np.float32)
    A = np.empty((128, S), dtype=np.float32)
    A[0:64] = cosT
    A[64:128] = cosT
    Bt = np.empty((128, S), dtype=np.float32)
    Bt[0:64] = sinT
    Bt[64:128] = sinT
    return A, Bt


def _rot_matrix():
    P = np.zeros((128, 128), dtype=np.float32)
    for j in range(64):
        P[64 + j, j] = -1.0
        P[j, 64 + j] = 1.0
    return P


def prep_inputs(inp):
    f32 = lambda a: np.ascontiguousarray(a, dtype=np.float32)
    x = np.asarray(inp["L0_attn_hidden_states"], dtype=np.float32)

    shared = {}
    shared["c_ident"] = f32(np.eye(128))
    tri = np.zeros((128, 128), dtype=np.float32)
    tri[np.triu_indices(128, k=1)] = FMIN
    shared["c_trimask"] = tri
    shared["c_keepT"] = f32(np.triu(np.ones((128, 128))))
    ra, rb = _rope_tables()
    shared["c_ropeA"] = ra
    shared["c_ropeB"] = rb
    shared["c_rot"] = _rot_matrix()
    shared["c_ones128"] = f32(np.ones((128, 1)))
    shared["c_ones1x128"] = f32(np.ones((1, 128)))

    wq_t = np.asarray(inp["L0_attn_w_q"], dtype=np.float32).T
    wk_t = np.asarray(inp["L0_attn_w_k"], dtype=np.float32).T
    wv_t = np.asarray(inp["L0_attn_w_v"], dtype=np.float32).T
    wo_t = np.asarray(inp["L0_attn_w_o"], dtype=np.float32).T
    mem = np.asarray(inp["L0_attn_memories"], dtype=np.float32)
    mn = np.asarray(inp["L0_attn_memory_norms"], dtype=np.float32)
    gate = np.asarray(inp["L0_attn_gate"], dtype=np.float32)
    shared["a_mn_all"] = f32(mn.transpose(2, 0, 1).reshape(128, 64))

    def head_tiles(wT, gh):
        sl = wT[:, gh * 128:(gh + 1) * 128]
        return f32(sl.reshape(16, 128, 128).transpose(1, 0, 2))

    gw = np.asarray(inp["L0_ffn_gate_w"], dtype=np.float32)
    shared["f_gwT"] = f32(gw.T.reshape(16, 128, 16).transpose(1, 0, 2))
    w1 = np.asarray(inp["L0_ffn_w1"], dtype=np.float32)
    w3 = np.asarray(inp["L0_ffn_w3"], dtype=np.float32)
    w2 = np.asarray(inp["L0_ffn_w2"], dtype=np.float32)

    def expT(w):
        E = w.shape[0]
        a = w.transpose(0, 2, 1).reshape(E, 16, 128, 11, 128)
        a = a.transpose(0, 2, 3, 1, 4).reshape(E, 128, 11, 2048)
        return np.ascontiguousarray(a, dtype=np.float32)

    shared["f_w1"] = expT(w1)
    shared["f_w3"] = expT(w3)
    a = w2.transpose(0, 2, 1).reshape(16, 11, 128, 16, 128)
    a = a.transpose(0, 2, 3, 1, 4).reshape(16, 128, 16, 1408)
    shared["f_w2"] = np.ascontiguousarray(a, dtype=np.float32)
    sw1 = np.asarray(inp["L0_ffn_sw1"], dtype=np.float32)
    sw3 = np.asarray(inp["L0_ffn_sw3"], dtype=np.float32)
    sw2 = np.asarray(inp["L0_ffn_sw2"], dtype=np.float32)
    a = sw1.T.reshape(16, 128, 22, 128).transpose(1, 2, 0, 3).reshape(128, 22, 2048)
    shared["f_sw1"] = np.ascontiguousarray(a, dtype=np.float32)
    a = sw3.T.reshape(16, 128, 22, 128).transpose(1, 2, 0, 3).reshape(128, 22, 2048)
    shared["f_sw3"] = np.ascontiguousarray(a, dtype=np.float32)
    a = sw2.T.reshape(22, 128, 16, 128).transpose(1, 2, 0, 3).reshape(128, 16, 2816)
    shared["f_sw2"] = np.ascontiguousarray(a, dtype=np.float32)

    wdkv = np.asarray(inp["L1_attn_W_DKV"], dtype=np.float32)
    shared["d_wdkv"] = f32(wdkv.reshape(16, 128, 512).transpose(1, 0, 2))
    shared["d_bdkv"] = f32(np.asarray(inp["L1_attn_b_DKV"], dtype=np.float32).reshape(4, 128).T)
    wuk = _perm_cols_per_head(np.asarray(inp["L1_attn_W_UK"], dtype=np.float32))
    wuv = np.asarray(inp["L1_attn_W_UV"], dtype=np.float32)
    buk = _perm_cols_per_head(np.asarray(inp["L1_attn_b_UK"], dtype=np.float32)[None, :])[0]
    buv = np.asarray(inp["L1_attn_b_UV"], dtype=np.float32)
    wdq = np.asarray(inp["L1_attn_W_DQ"], dtype=np.float32)
    shared["d_wdq"] = f32(wdq.reshape(16, 128, 1536).transpose(1, 0, 2))
    shared["d_bdq"] = f32(np.asarray(inp["L1_attn_b_DQ"], dtype=np.float32).reshape(12, 128).T)
    wuq = _perm_cols_per_head(np.asarray(inp["L1_attn_W_UQ"], dtype=np.float32))
    buq = _perm_cols_per_head(np.asarray(inp["L1_attn_b_UQ"], dtype=np.float32)[None, :])[0]
    wkr = _perm_cols_per_head(np.asarray(inp["L1_attn_W_KR"], dtype=np.float32))
    bkr = _perm_cols_per_head(np.asarray(inp["L1_attn_b_KR"], dtype=np.float32)[None, :])[0]
    wo1 = np.asarray(inp["L1_attn_W_O"], dtype=np.float32)
    shared["d_bo"] = f32(np.asarray(inp["L1_attn_b_O"], dtype=np.float32).reshape(16, 128).T)

    gw1 = np.asarray(inp["L1_ffn_gate_w"], dtype=np.float32)
    shared["e_gwT"] = f32(gw1.T.reshape(16, 128, 8).transpose(1, 0, 2))
    la = np.asarray(inp["L1_ffn_lora_A"], dtype=np.float32)
    Ast = la.reshape(128, 2048).T
    shared["e_A"] = f32(Ast.reshape(16, 128, 128).transpose(1, 0, 2))
    lb = np.asarray(inp["L1_ffn_lora_B"], dtype=np.float32)
    Bst = lb.transpose(0, 2, 1).reshape(128, 2048)
    shared["e_B"] = f32(Bst.reshape(128, 16, 128))
    e16 = np.zeros((8, 128), dtype=np.float32)
    for e in range(8):
        e16[e, e * 16:(e + 1) * 16] = 1.0
    shared["e_E16"] = e16

    in_maps = []
    for c in range(NC):
        b, g = c // 2, c % 2
        m = dict(shared)
        xb = x[b]
        m["x_fm"] = f32(xb.T.reshape(16, 128, 2048).transpose(1, 0, 2))
        m["x_own"] = f32(xb[g * TOK:(g + 1) * TOK].T.reshape(16, 128, TOK).transpose(1, 0, 2))
        heads = list(range(8 * g, 8 * g + 8))
        m["a_wq"] = np.stack([head_tiles(wq_t, gh) for gh in heads])
        m["a_wk"] = np.stack([head_tiles(wk_t, gh) for gh in heads])
        m["a_wv"] = np.stack([head_tiles(wv_t, gh) for gh in heads])
        me = np.zeros((8, 128, 4, 129), dtype=np.float32)
        for i, gh in enumerate(heads):
            for mm_ in range(4):
                me[i, :, mm_, :128] = mem[mm_, gh]
                me[i, :, mm_, 128] = mn[mm_, gh]
        m["a_mem"] = me
        m["a_gate8"] = f32(gate[8 * g:8 * g + 8][None, :])
        wo_rows = wo_t[g * 1024:(g + 1) * 1024]
        m["a_wo"] = np.ascontiguousarray(wo_rows.reshape(8, 128, 16, 128).transpose(1, 0, 2, 3), dtype=np.float32)
        m["d_wuk"] = np.stack([f32(wuk[:, gh * 128:(gh + 1) * 128].reshape(4, 128, 128).transpose(1, 0, 2)) for gh in heads])
        m["d_wuv"] = np.stack([f32(wuv[:, gh * 128:(gh + 1) * 128].reshape(4, 128, 128).transpose(1, 0, 2)) for gh in heads])
        m["d_buk"] = f32(np.stack([buk[gh * 128:(gh + 1) * 128] for gh in heads], axis=1))
        m["d_buv_row"] = f32(np.stack([buv[gh * 128:(gh + 1) * 128] for gh in heads], axis=0))
        m["d_wuq"] = np.stack([f32(wuq[:, gh * 128:(gh + 1) * 128].reshape(12, 128, 128).transpose(1, 0, 2)) for gh in heads])
        m["d_buq"] = f32(np.stack([buq[gh * 128:(gh + 1) * 128] for gh in heads], axis=1))
        m["d_wkr"] = np.stack([f32(wkr[:, gh * 128:(gh + 1) * 128].reshape(16, 128, 128).transpose(1, 0, 2)) for gh in heads])
        m["d_bkr"] = f32(np.stack([bkr[gh * 128:(gh + 1) * 128] for gh in heads], axis=1))
        wo1_rows = wo1[g * 1024:(g + 1) * 1024]
        m["d_wo"] = np.ascontiguousarray(wo1_rows.reshape(8, 128, 16, 128).transpose(1, 0, 2, 3), dtype=np.float32)
        in_maps.append(m)
    return in_maps



class Sem:
    def __init__(self, ctx, nc, name):
        self.name = name
        self.h = ctx.enter_context(nc.semaphore(name))
        self.n = 0

    def inc(self, instr, k=1):
        instr.then_inc(self.h, k)
        self.n += k
        return self.n


class Stream:

    def __init__(self, k, name):
        self.k = k
        self.sem = Sem(k.ctx, k.nc, name)

    def dma(self, queue, out, in_, serial=False):
        if serial and self.sem.n > 0:
            self.k.wait(queue, self.sem, self.sem.n)
        getattr(self.k.nc, queue).dma_start(out=out, in_=in_).then_inc(self.sem.h, 16)
        self.sem.n += 16
        return self.sem.n


class Ring:

    def __init__(self, k, name, tiles):
        self.k = k
        self.tiles = tiles
        self.streams = [k.stream(f"{name}{i}") for i in range(len(tiles))]
        self.guard = [None] * len(tiles)
        self.i = 0

    def load(self, queue, src, out_view=None):
        s = self.i % len(self.tiles)
        self.i += 1
        if self.guard[s] is not None:
            self.k.wait(queue, self.guard[s][0], self.guard[s][1])
        dst = self.tiles[s] if out_view is None else out_view(s)
        tok = self.streams[s].dma(queue, dst, src)
        return s, self.streams[s].sem, tok

    def consumed(self, s, sem, val):
        self.guard[s] = (sem, val)


class SerialEngine:

    def __init__(self, k, eng, sem):
        self._k = k
        self._eng = eng
        self._sem = sem

    def __getattr__(self, attr):
        k = self._k
        f = getattr(getattr(k.nc, self._eng), attr)

        def wrapped(*a, **kw):
            k.wait(self._eng, self._sem, k.sems[self._sem].n)
            instr = f(*a, **kw)
            k.sems[self._sem].inc(instr)
            return instr

        return wrapped


class K:
    def __init__(self, ctx, nc):
        self.ctx = ctx
        self.nc = nc
        self.sems = {}
        for name in ["MM", "DV", "AC", "CC"]:
            self.sems[name] = Sem(ctx, nc, "s_" + name)
        self.all_sems = list(self.sems.values())
        self.waited = {e: {} for e in ["tensor", "vector", "scalar", "gpsimd", "sync"]}

    def new_sem(self, name):
        s = Sem(self.ctx, self.nc, name)
        self.all_sems.append(s)
        return s

    def stream(self, name):
        st = Stream(self, name)
        self.all_sems.append(st.sem)
        return st

    def _sem(self, s):
        return self.sems[s] if isinstance(s, str) else s

    def wait(self, eng_name, sem, val):
        if val <= 0:
            return
        sem = self._sem(sem)
        w = self.waited[eng_name]
        if w.get(sem.name, 0) >= val:
            return
        getattr(self.nc, eng_name).wait_ge(sem.h, val)
        w[sem.name] = val

    def fence_all(self):
        for e in self.waited:
            for s in self.all_sems:
                self.wait(e, s, s.n)


def build_program():
    import contextlib
    ctx = contextlib.ExitStack()
    nc = bass.Bass("TRN2", target_bir_lowering=False, debug=False)

    def din(name, shape):
        return nc.dram_tensor(name, list(shape), F32, kind="ExternalInput").ap()

    def dout(name, shape):
        return nc.dram_tensor(name, list(shape), F32, kind="ExternalOutput").ap()

    def dint(name, shape):
        return nc.dram_tensor(name, list(shape), F32, kind="Internal").ap()

    x_fm = din("x_fm", (128, 16, 2048))
    x_own = din("x_own", (128, 16, TOK))
    c_ident = din("c_ident", (128, 128))
    c_trimask = din("c_trimask", (128, 128))
    c_keepT = din("c_keepT", (128, 128))
    c_ropeA = din("c_ropeA", (128, 2048))
    c_ropeB = din("c_ropeB", (128, 2048))
    c_rot = din("c_rot", (128, 128))
    c_ones128 = din("c_ones128", (128, 1))
    c_ones1x128 = din("c_ones1x128", (1, 128))
    a_wq = din("a_wq", (8, 128, 16, 128))
    a_wk = din("a_wk", (8, 128, 16, 128))
    a_wv = din("a_wv", (8, 128, 16, 128))
    a_mem = din("a_mem", (8, 128, 4, 129))
    a_mn_all = din("a_mn_all", (128, 64))
    a_gate8 = din("a_gate8", (1, 8))
    a_wo = din("a_wo", (128, 8, 16, 128))
    f_gwT = din("f_gwT", (128, 16, 16))
    f_w1 = din("f_w1", (16, 128, 11, 2048))
    f_w3 = din("f_w3", (16, 128, 11, 2048))
    f_w2 = din("f_w2", (16, 128, 16, 1408))
    f_sw1 = din("f_sw1", (128, 22, 2048))
    f_sw3 = din("f_sw3", (128, 22, 2048))
    f_sw2 = din("f_sw2", (128, 16, 2816))
    d_wdkv = din("d_wdkv", (128, 16, 512))
    d_bdkv = din("d_bdkv", (128, 4))
    d_wdq = din("d_wdq", (128, 16, 1536))
    d_bdq = din("d_bdq", (128, 12))
    d_wuk = din("d_wuk", (8, 128, 4, 128))
    d_wuv = din("d_wuv", (8, 128, 4, 128))
    d_buk = din("d_buk", (128, 8))
    d_buv_row = din("d_buv_row", (8, 128))
    d_wuq = din("d_wuq", (8, 128, 12, 128))
    d_buq = din("d_buq", (128, 8))
    d_wkr = din("d_wkr", (8, 128, 16, 128))
    d_bkr = din("d_bkr", (128, 8))
    d_wo = din("d_wo", (128, 8, 16, 128))
    d_bo = din("d_bo", (128, 16))
    e_gwT = din("e_gwT", (128, 16, 8))
    e_A = din("e_A", (128, 16, 128))
    e_B = din("e_B", (128, 16, 128))
    e_E16 = din("e_E16", (8, 128))

    y = dout("y", (16, 128, TOK))

    cc0_in = dint("cc0_in", (2, 16, 128, TOK))
    cc0_out = dint("cc0_out", (16, 128, TOK))
    cc1_in = dint("cc1_in", (16, 128, TOK))
    cc1_out = dint("cc1_out", (2, 16, 128, TOK))
    cc2_in = dint("cc2_in", (2, 16, 128, TOK))
    cc2_out = dint("cc2_out", (16, 128, TOK))
    cq_hbm = dint("cq_hbm", (128, 12, 2048))

    taps = {}
    if DEBUG_TAPS:
        taps["tap_ctx0"] = dout("tap_ctx0", (128, 8, 2048))
        taps["tap_a0"] = dout("tap_a0", (16, 128, TOK))
        taps["tap_x1"] = dout("tap_x1", (16, 128, TOK))
        taps["tap_x2"] = dout("tap_x2", (16, 128, TOK))
        taps["tap_x3"] = dout("tap_x3", (16, 128, TOK))

    with ctx:
        sb = lambda name, shape: ctx.enter_context(nc.sbuf_tensor(name, list(shape), F32))
        pst = lambda name, shape: ctx.enter_context(nc.psum_tensor(name, list(shape), F32))

        k = K(ctx, nc)
        S_MM, S_DV, S_AC, S_CC = (k.sems[n] for n in ["MM", "DV", "AC", "CC"])
        V = SerialEngine(k, "vector", "DV")
        A = SerialEngine(k, "scalar", "AC")
        T = nc.tensor

        AR_RES = sb("AR_RES", (128, 16, TOK))
        AR_BIG = sb("AR_BIG", (128, 16384))
        AR_W = sb("AR_W", (128, 8192))
        RING = sb("RING", (128, 5632))
        ident = sb("ident", (128, 128))
        trimask = sb("trimask", (128, 128))
        keepT = sb("keepT", (128, 128))
        rotm = sb("rotm", (128, 128))
        bias_sb = sb("bias_sb", (128, 64))
        bvrow = sb("bvrow", (1, 128))
        gwT2_sb = sb("gwT2_sb", (128, 16, 8))
        e16_sb = sb("e16_sb", (8, 128))
        ones128 = sb("ones128", (128, 1))
        ones1x128 = sb("ones1x128", (1, 128))
        rowbuf = sb("rowbuf", (1, 1664))
        stat_tmp = sb("stat_tmp", (128, 1024))
        xbar = sb("xbar", (128, 16))
        xbar_t = sb("xbar_t", (128, 16))
        gwT_sb = sb("gwT_sb", (128, 16, 16))
        router = sb("router", (128, 96))
        westore = sb("westore", (128, 64))
        zcol = sb("zcol", (128, 1))
        small = sb("small", (1, 64))
        gate_sb = sb("gate_sb", (1, 8))
        sig8 = sb("sig8", (1, 8))
        mn_all = sb("mn_all", (128, 64))
        emask = sb("emask", (1, 8))
        cgrep = sb("cgrep", (128, 8))

        xt = [RING[:, i * 512:(i + 1) * 512] for i in range(11)]

        PS = [pst(f"ps{i}", (128, 512)) for i in range(8)]

        def ln_fm(src_tiles, dst_tiles, psA, psB, nt):
            k.wait("tensor", "DV", S_DV.n)
            k.wait("tensor", "AC", S_AC.n)
            k.wait("scalar", "DV", S_DV.n)
            mm_tok = {}
            for c in range(NCH):
                slot = c % 2
                if c >= 2:
                    k.wait("scalar", "MM", mm_tok[slot])
                A.activation(out=stat_tmp[:, slot * nt:(slot + 1) * nt], in_=src_tiles(c), func=AF.Square)
                ac = S_AC.n
                T.matmul(psA[0:1, 0:nt], ones128[:], src_tiles(c), start=(c == 0), stop=(c == NCH - 1))
                k.wait("tensor", "AC", ac)
                mm = S_MM.inc(T.matmul(psB[0:1, 0:nt], ones128[:], stat_tmp[:, slot * nt:(slot + 1) * nt],
                                       start=(c == 0), stop=(c == NCH - 1)))
                mm_tok[slot] = mm
            k.wait("vector", "MM", S_MM.n)
            mean = rowbuf[0:1, 0:nt]
            var = rowbuf[0:1, nt:2 * nt]
            rstd = rowbuf[0:1, 2 * nt:3 * nt]
            V.tensor_scalar_mul(mean, psA[0:1, 0:nt], 1.0 / D)
            V.tensor_scalar_mul(var, psB[0:1, 0:nt], 1.0 / D)
            V.tensor_tensor(rstd, mean, mean, op=ALU.mult)
            V.tensor_tensor(var, var, rstd, op=ALU.subtract)
            V.tensor_scalar_add(var, var, EPS_LN)
            dv = S_DV.n
            k.wait("scalar", "DV", dv)
            A.activation(out=var, in_=var, func=AF.Sqrt)
            k.wait("vector", "AC", S_AC.n)
            V.reciprocal(rstd, var)
            V.tensor_tensor(mean, mean, rstd, op=ALU.mult)
            V.tensor_scalar_mul(mean, mean, -1.0)
            k.wait("tensor", "DV", S_DV.n)
            S_MM.inc(T.matmul(psA[:, 0:nt], ones1x128[:], rstd, start=True, stop=True))
            mmB = S_MM.inc(T.matmul(psB[:, 0:nt], ones1x128[:], mean, start=True, stop=True))
            k.wait("vector", "MM", mmB)
            for c in range(NCH):
                V.tensor_tensor(stat_tmp[:, 0:nt], src_tiles(c), psA[:, 0:nt], op=ALU.mult)
                V.tensor_tensor(dst_tiles(c), stat_tmp[:, 0:nt], psB[:, 0:nt], op=ALU.add)

        st_c = k.stream("stc")
        st_wq = k.stream("stwq")
        st_wk = k.stream("stwk")
        st_wv = k.stream("stwv")
        st_mem = k.stream("stmem")
        st_wo = [k.stream("stwo0"), k.stream("stwo1")]
        st_ob = [k.stream(f"stob{i}") for i in range(4)]
        st_tap = k.stream("sttap")
        st_y = k.stream("sty")
        st_cc1 = k.stream("stcc1")
        xring = Ring(k, "xr", xt)

        st_c.dma("sync", ident[:], c_ident[:])
        st_c.dma("sync", trimask[:], c_trimask[:])
        st_c.dma("sync", keepT[:], c_keepT[:])
        st_c.dma("sync", rotm[:], c_rot[:])
        st_c.dma("sync", bias_sb[:, 0:4], d_bdkv[:])
        st_c.dma("sync", bias_sb[:, 4:16], d_bdq[:])
        st_c.dma("sync", bias_sb[:, 16:24], d_buk[:])
        st_c.dma("sync", bias_sb[:, 24:32], d_bkr[:])
        st_c.dma("sync", bias_sb[:, 32:40], d_buq[:])
        st_c.dma("sync", bias_sb[:, 40:56], d_bo[:])
        st_c.dma("sync", gwT2_sb[:], e_gwT[:])
        st_c.dma("sync", e16_sb[:], e_E16[:])
        st_c.dma("sync", ones128[:], c_ones128[:])
        st_c.dma("sync", ones1x128[:], c_ones1x128[:])
        st_c.dma("sync", mn_all[:], a_mn_all[:])
        st_c.dma("sync", gwT_sb[:], f_gwT[:])
        const_tok = st_c.dma("sync", gate_sb[:], a_gate8[:])

        k.wait("scalar", st_c.sem, const_tok)
        A.activation(out=sig8[:], in_=gate_sb[:], func=AF.Sigmoid)
        sig_ac = S_AC.n
        V.memset(zcol[:], 0.0)
        k.wait("tensor", st_c.sem, const_tok)
        mm = S_MM.inc(T.matmul(PS[0][0:1, 0:64], ones128[:], mn_all[:], start=True, stop=True))
        k.wait("vector", "MM", mm)
        for m_ in range(4):
            V.reduce_sum(small[0:1, m_:m_ + 1], PS[0][0:1, m_ * 16:(m_ + 1) * 16], axis=AX.X)
        V.tensor_scalar(emask[0:1, 0:4], small[0:1, 0:4], 1e-6, -1e38, op0=ALU.is_lt, op1=ALU.mult)
        V.tensor_scalar(small[0:1, 4:8], small[0:1, 0:4], 1e-6, None, op0=ALU.is_lt)
        V.tensor_reduce(small[0:1, 12:13], small[0:1, 4:8], axis=AX.X, op=ALU.min)
        V.tensor_scalar(small[0:1, 13:14], small[0:1, 12:13], -1.0, 1.0, op0=ALU.mult, op1=ALU.add)

        for blk in range(4):
            for c in range(NCH):
                slot, sem, ld = xring.load("sync", x_fm[:, c, blk * 512:(blk + 1) * 512])
                k.wait("vector", sem, ld)
                V.reduce_sum(xbar_t[:, c:c + 1], xt[slot], axis=AX.X)
                xring.consumed(slot, S_DV, S_DV.n)
            if blk == 0:
                V.tensor_copy(xbar[:], xbar_t[:])
            else:
                V.tensor_tensor(xbar[:], xbar[:], xbar_t[:], op=ALU.add)
        a0_done_dv = S_DV.n

        arw = AR_RES[:].rearrange("p c t -> p (c t)")
        wq_t_ = arw[:, 0:2048].rearrange("p (c i) -> p c i", c=16)
        wk_t_ = arw[:, 2048:4096].rearrange("p (c i) -> p c i", c=16)
        wv_t_ = arw[:, 4096:6144].rearrange("p (c i) -> p c i", c=16)
        mem_t = arw[:, 6144:6660].rearrange("p (m i) -> p m i", m=4)
        qp_blk = arw[:, 6660:7172]
        kp_blk = arw[:, 7172:7684]
        vext_blk = arw[:, 7684:8200].rearrange("p (s i) -> p s i", s=4)
        kptok = arw[:, 8200:8328]
        sT_t = arw[:, 8328:8456]
        sbS = arw[:, 8456:8585]
        macc = arw[:, 8585:8713]
        mtmp = arw[:, 8713:8841]
        lout = arw[:, 8841:8969]
        qbar_s = arw[:, 8969:8970]
        ctx_all = AR_BIG[:].rearrange("p (h t) -> p h t", h=8)

        k.wait("sync", "DV", a0_done_dv)
        for h0 in range(HPC):
            tq = st_wq.dma("sync", wq_t_, a_wq[h0])
            tk = st_wk.dma("sync", wk_t_, a_wk[h0])
            tv = st_wv.dma("sync", wv_t_, a_wv[h0])
            tm = st_mem.dma("sync", mem_t, a_mem[h0])

            k.wait("tensor", st_wq.sem, tq)
            k.wait("tensor", st_wk.sem, tk)
            k.wait("tensor", st_wv.sem, tv)
            k.wait("tensor", st_mem.sem, tm)
            k.wait("tensor", "DV", a0_done_dv)
            for c in range(NCH):
                mi = T.matmul(PS[6][0:128, 0:1], wq_t_[:, c, :], xbar[:, c:c + 1],
                              start=(c == 0), stop=(c == NCH - 1))
            mm = S_MM.inc(mi)
            k.wait("vector", "MM", mm)
            V.tensor_copy(qbar_s, PS[6][0:128, 0:1])
            k.wait("tensor", "DV", S_DV.n)
            mm = S_MM.inc(T.matmul(PS[6][0:1, 0:4], qbar_s, mem_t[:, :, 128], start=True, stop=True))
            k.wait("vector", "MM", mm)
            r0 = rowbuf[0:1, 0:4]
            V.tensor_scalar_mul(r0, PS[6][0:1, 0:4], SCALE_L0 / S)
            V.tensor_tensor(r0, r0, emask[0:1, 0:4], op=ALU.add)
            V.reduce_max(rowbuf[0:1, 4:5], r0, axis=AX.X, negate=True)
            k.wait("scalar", "DV", S_DV.n)
            A.activation(out=r0, in_=r0, func=AF.Exp, bias=rowbuf[0:1, 4:5], scale=1.0,
                         accum_out=rowbuf[0:1, 5:6])
            k.wait("vector", "AC", S_AC.n)
            V.reciprocal(rowbuf[0:1, 6:7], rowbuf[0:1, 5:6])
            V.tensor_scalar(r0, r0, rowbuf[0:1, 6:7], small[0:1, 13:14], op0=ALU.mult, op1=ALU.mult)
            V.tensor_scalar(rowbuf[0:1, 8:12], r0, sig8[0:1, h0:h0 + 1], None, op0=ALU.mult)
            V.tensor_scalar(rowbuf[0:1, 7:8], sig8[0:1, h0:h0 + 1], -1.0, 1.0, op0=ALU.mult, op1=ALU.add)
            k.wait("tensor", "DV", S_DV.n)
            mm = S_MM.inc(T.matmul(PS[6][:, 0:8], ones1x128[:], rowbuf[0:1, 4:12], start=True, stop=True))
            k.wait("vector", "MM", mm)
            V.tensor_copy(cgrep[:, 0:8], PS[6][:, 0:8])
            cg_col = lambda m_: cgrep[:, 4 + m_:5 + m_]
            g1m_col = cgrep[:, 3:4]

            V.memset(sbS[:, :], 0.0)
            V.memset(vext_blk[:, :, 128], 1.0)

            prev_evac_dv = S_DV.n
            for blk in range(NBLK):
                tb = blk * 512
                k.wait("tensor", "DV", prev_evac_dv)
                for c in range(NCH):
                    slot, sem, ld = xring.load("sync", x_fm[:, c, tb:tb + 512])
                    k.wait("tensor", sem, ld)
                    T.matmul(PS[0][:, :], wq_t_[:, c, :], xt[slot], start=(c == 0), stop=(c == NCH - 1))
                    T.matmul(PS[1][:, :], wk_t_[:, c, :], xt[slot], start=(c == 0), stop=(c == NCH - 1))
                    for s_ in range(4):
                        mi = T.matmul(PS[2 + s_][:, 0:128], xt[slot][:, s_ * 128:s_ * 128 + 128],
                                      wv_t_[:, c, :], start=(c == 0), stop=(c == NCH - 1))
                    xring.consumed(slot, S_MM, S_MM.inc(mi))
                mm_qkv = S_MM.n
                k.wait("vector", "MM", mm_qkv)
                V.tensor_scalar_min(qp_blk, PS[0][:, :], 0.0)
                V.tensor_scalar_min(kp_blk, PS[1][:, :], 0.0)
                k.wait("scalar", "DV", S_DV.n)
                A.activation(out=qp_blk, in_=qp_blk, func=AF.Exp)
                A.activation(out=kp_blk, in_=kp_blk, func=AF.Exp)
                k.wait("vector", "AC", S_AC.n)
                V.scalar_tensor_tensor(qp_blk, PS[0][:, :], 0.0, qp_blk, op0=ALU.max, op1=ALU.add)
                V.scalar_tensor_tensor(kp_blk, PS[1][:, :], 0.0, kp_blk, op0=ALU.max, op1=ALU.add)
                for s_ in range(4):
                    V.tensor_copy(vext_blk[:, s_, 0:128], PS[2 + s_][:, 0:128])
                prev_evac_dv = S_DV.n

                for s_ in range(4):
                    k.wait("tensor", "DV", prev_evac_dv)
                    for m_ in range(4):
                        pm = PS[3 + (m_ % 2)][:, (m_ // 2) * 129:(m_ // 2) * 129 + 129]
                        T.matmul(pm, qp_blk[:, s_ * 128:s_ * 128 + 128], mem_t[:, m_, :],
                                 start=True, stop=True)
                    T.transpose(PS[5][:, 0:128], kp_blk[:, s_ * 128:s_ * 128 + 128], ident[:])
                    mi = T.matmul(PS[5][:, 128:256], kp_blk[:, s_ * 128:s_ * 128 + 128],
                                  qp_blk[:, s_ * 128:s_ * 128 + 128], start=True, stop=True)
                    mm_t1 = S_MM.inc(mi)
                    k.wait("vector", "MM", mm_t1)
                    V.tensor_copy(kptok, PS[5][:, 0:128])
                    V.tensor_tensor(sT_t, PS[5][:, 128:256], keepT[:], op=ALU.mult)
                    dv1 = S_DV.n
                    k.wait("tensor", "DV", dv1)
                    T.matmul(PS[6][:, 0:129], qp_blk[:, s_ * 128:s_ * 128 + 128], sbS[:, :],
                             start=True, stop=False)
                    T.matmul(PS[6][:, 0:129], sT_t, vext_blk[:, s_, :], start=False, stop=True)
                    n_ = blk * 4 + s_
                    mi = T.matmul(PS[7][:, 0:129], kptok, vext_blk[:, s_, :],
                                  start=True, stop=True)
                    mm_loc = S_MM.inc(mi)
                    k.wait("vector", "MM", mm_loc)
                    V.tensor_tensor(sbS[:, :], sbS[:, :], PS[7][:, 0:129], op=ALU.add)
                    V.tensor_scalar_max(stat_tmp[:, 0:1], PS[6][:, 128:129], EPS_L0)
                    for m_ in range(4):
                        pm = PS[3 + (m_ % 2)][:, (m_ // 2) * 129:(m_ // 2) * 129 + 129]
                        V.tensor_scalar_max(stat_tmp[:, 2 + m_:3 + m_], pm[:, 128:129], EPS_L0)
                    V.reciprocal(stat_tmp[:, 1:2], stat_tmp[:, 0:1])
                    for m_ in range(4):
                        V.reciprocal(stat_tmp[:, 6 + m_:7 + m_], stat_tmp[:, 2 + m_:3 + m_])
                    V.tensor_scalar(lout, PS[6][:, 0:128], stat_tmp[:, 1:2], g1m_col, op0=ALU.mult, op1=ALU.mult)
                    for m_ in range(4):
                        pm = PS[3 + (m_ % 2)][:, (m_ // 2) * 129:(m_ // 2) * 129 + 129]
                        if m_ == 0:
                            V.tensor_scalar(macc, pm[:, 0:128], stat_tmp[:, 6:7], cg_col(0), op0=ALU.mult, op1=ALU.mult)
                        else:
                            V.tensor_scalar(mtmp, pm[:, 0:128], stat_tmp[:, 6 + m_:7 + m_], cg_col(m_), op0=ALU.mult, op1=ALU.mult)
                            V.tensor_tensor(macc, macc, mtmp, op=ALU.add)
                    V.tensor_tensor(lout, lout, macc, op=ALU.add)
                    dv6 = S_DV.n
                    k.wait("tensor", "DV", dv6)
                    mi = T.transpose(PS[5][:, 256:384], lout, ident[:])
                    mm_ot = S_MM.inc(mi)
                    k.wait("vector", "MM", mm_ot)
                    V.tensor_copy(ctx_all[:, h0, tb + s_ * 128:tb + s_ * 128 + 128], PS[5][:, 256:384])
                    prev_evac_dv = S_DV.n
            k.wait("sync", "MM", S_MM.n)
            k.wait("sync", "DV", S_DV.n)

        if DEBUG_TAPS:
            k.wait("gpsimd", "DV", S_DV.n)
            st_tap.dma("gpsimd", taps["tap_ctx0"][:], ctx_all[:, :, :], serial=True)

        arw2 = AR_RES[:].rearrange("p c t -> p (c t)")
        wo_sl = [arw2[:, 0:1024].rearrange("p (h i) -> p h i", h=8),
                 arw2[:, 1024:2048].rearrange("p (h i) -> p h i", h=8)]
        obuf = [arw2[:, 2048 + i * 512: 2048 + (i + 1) * 512] for i in range(4)]
        k.wait("sync", "MM", S_MM.n)
        k.wait("sync", "DV", S_DV.n)
        wo_free = {0: 0, 1: 0}
        ob_free = {i: 0 for i in range(4)}
        ps_free = {0: 0, 1: 0}
        for oc in range(16):
            slot = oc % 2
            k.wait("sync", "MM", wo_free[slot])
            ldtok = st_wo[slot].dma("sync", wo_sl[slot], a_wo[:, :, oc, :])
            k.wait("tensor", st_wo[slot].sem, ldtok)
            for tb4 in range(4):
                bslot = (oc * 4 + tb4) % 2
                k.wait("tensor", "DV", ps_free[bslot])
                for h0 in range(8):
                    mi = T.matmul(PS[bslot][:, :], wo_sl[slot][:, h0, :], ctx_all[:, h0, tb4 * 512:(tb4 + 1) * 512],
                                  start=(h0 == 0), stop=(h0 == 7))
                mm = S_MM.inc(mi)
                oslot = (oc * 4 + tb4) % 4
                k.wait("vector", "MM", mm)
                k.wait("vector", st_ob[oslot].sem, ob_free[oslot])
                V.tensor_copy(obuf[oslot], PS[bslot][:, :])
                ps_free[bslot] = S_DV.n
                k.wait("gpsimd", "DV", S_DV.n)
                t0 = tb4 * 512
                ob_free[oslot] = st_ob[oslot].dma("gpsimd", cc0_in[t0 // TOK, oc, :, t0 % TOK:t0 % TOK + 512], obuf[oslot])
            wo_free[slot] = S_MM.n
        for i in range(4):
            k.wait("gpsimd", st_ob[i].sem, st_ob[i].sem.n)
        cc = S_CC.inc(nc.gpsimd.collective_compute(
            "ReduceScatter", ALU.add,
            replica_groups=[[0, 1], [2, 3], [4, 5], [6, 7]],
            ins=[cc0_in[:]], outs=[cc0_out[:]]))

        k.wait("sync", "CC", cc)
        k.wait("vector", "MM", S_MM.n)
        for i in range(4):
            k.wait("vector", st_ob[i].sem, st_ob[i].sem.n)
        arst = AR_RES[:]
        for tb2 in range(2):
            for c in range(NCH):
                s0, sem0, ld0 = xring.load("sync", x_own[:, c, tb2 * 512:(tb2 + 1) * 512])
                s1, sem1, ld1 = xring.load("sync", cc0_out[c, :, tb2 * 512:(tb2 + 1) * 512])
                k.wait("vector", sem0, ld0)
                k.wait("vector", sem1, ld1)
                V.tensor_tensor(arst[:, c, tb2 * 512:(tb2 + 1) * 512], xt[s0], xt[s1], op=ALU.add)
                xring.consumed(s0, S_DV, S_DV.n)
                if DEBUG_TAPS:
                    k.wait("gpsimd", "DV", S_DV.n)
                    ttok = st_tap.dma("gpsimd", taps["tap_a0"][c, :, tb2 * 512:(tb2 + 1) * 512], xt[s1], serial=True)
                    xring.consumed(s1, st_tap.sem, ttok)
                else:
                    xring.consumed(s1, S_DV, S_DV.n)
        for tb2 in range(2):
            ln_fm(src_tiles=lambda c, tb2=tb2: arst[:, c, tb2 * 512:(tb2 + 1) * 512],
                  dst_tiles=lambda c, tb2=tb2: arst[:, c, tb2 * 512:(tb2 + 1) * 512],
                  psA=PS[0], psB=PS[1], nt=512)
        if DEBUG_TAPS:
            k.wait("gpsimd", "DV", S_DV.n)
            for c in range(NCH):
                st_tap.dma("gpsimd", taps["tap_x1"][c], arst[:, c, :], serial=True)

        y_t = AR_BIG[:].rearrange("p (o t) -> p o t", o=16)
        hh = AR_W[:, 0:5632].rearrange("p (i t) -> p i t", i=11)
        w1fl = [RING[:, 0:1024], RING[:, 1024:2048]]
        w3fl = [RING[:, 2048:3072], RING[:, 3072:4096]]
        w2fl = [RING[:, 4096:4864], RING[:, 4864:5632]]
        w1v = [t.rearrange("p (c i) -> p c i", c=8) for t in w1fl]
        w3v = [t.rearrange("p (c i) -> p c i", c=8) for t in w3fl]
        w2v = [t.rearrange("p (c i) -> p c i", c=6) for t in w2fl]
        w1r = Ring(k, "w1r", w1fl)
        w3r = Ring(k, "w3r", w3fl)
        w2r = Ring(k, "w2r", w2fl)
        werep = stat_tmp[:, 512:1024]
        werow0 = rowbuf[0:1, 0:512]

        units = [("sh", 0), ("sh", 1)] + [("exp", e) for e in range(16)]

        k.wait("sync", "DV", S_DV.n)
        k.wait("sync", "MM", S_MM.n)
        if DEBUG_TAPS:
            k.wait("sync", st_tap.sem, st_tap.sem.n)
        ps01_free = 0
        ps_y_free = {2: 0, 3: 0}
        for th in range(2):
            t0 = th * 512
            x1h = lambda c: arst[:, c, t0:t0 + 512]
            for st_ in range(4):
                k.wait("tensor", "DV", S_DV.n)
                k.wait("tensor", "AC", S_AC.n)
                for c in range(NCH):
                    mi = T.matmul(PS[4][:, 0:16], arst[:, c, t0 + st_ * 128:t0 + st_ * 128 + 128],
                                  gwT_sb[:, c, :], start=(c == 0), stop=(c == NCH - 1))
                mm = S_MM.inc(mi)
                k.wait("vector", "MM", mm)
                V.reduce_max(router[:, 64:65], PS[4][:, 0:16], axis=AX.X, negate=True)
                k.wait("scalar", "DV", S_DV.n)
                A.activation(out=router[:, 0:16], in_=PS[4][:, 0:16], func=AF.Exp,
                             bias=router[:, 64:65], scale=1.0, accum_out=router[:, 65:66])
                k.wait("vector", "AC", S_AC.n)
                V.reciprocal(router[:, 66:67], router[:, 65:66])
                V.tensor_scalar_mul(router[:, 0:16], router[:, 0:16], router[:, 66:67])
                V.reduce_max(router[:, 67:68], router[:, 0:16], axis=AX.X)
                V.tensor_scalar(router[:, 16:32], router[:, 0:16], router[:, 67:68], None, op0=ALU.is_ge)
                V.tensor_tensor_scan(router[:, 32:48], router[:, 16:32], router[:, 16:32], zcol[:],
                                     op0=ALU.add, op1=ALU.bypass)
                V.tensor_scalar(router[:, 48:64], router[:, 32:48], 1.0, None, op0=ALU.is_equal)
                V.tensor_tensor(router[:, 16:32], router[:, 16:32], router[:, 48:64], op=ALU.mult)
                V.scalar_tensor_tensor(router[:, 32:48], router[:, 16:32], -2.0, router[:, 0:16],
                                       op0=ALU.mult, op1=ALU.add)
                V.reduce_max(router[:, 68:69], router[:, 32:48], axis=AX.X)
                V.tensor_scalar(router[:, 48:64], router[:, 32:48], router[:, 68:69], None, op0=ALU.is_ge)
                V.tensor_tensor_scan(router[:, 32:48], router[:, 48:64], router[:, 48:64], zcol[:],
                                     op0=ALU.add, op1=ALU.bypass)
                V.tensor_scalar(router[:, 32:48], router[:, 32:48], 1.0, None, op0=ALU.is_equal)
                V.tensor_tensor(router[:, 48:64], router[:, 48:64], router[:, 32:48], op=ALU.mult)
                V.tensor_scalar(router[:, 16:32], router[:, 16:32], router[:, 67:68], None, op0=ALU.mult)
                V.tensor_scalar(router[:, 48:64], router[:, 48:64], router[:, 68:69], None, op0=ALU.mult)
                V.tensor_tensor(westore[:, st_ * 16:(st_ + 1) * 16], router[:, 16:32], router[:, 48:64], op=ALU.add)
            dv_we = S_DV.n

            for ui, (ukind, ue) in enumerate(units):
                first_unit = (ui == 0)
                if ukind == "exp":
                    k.wait("tensor", "DV", S_DV.n)
                    k.wait("tensor", "AC", S_AC.n)
                    for st_ in range(4):
                        mi = T.transpose(PS[5][0:1, st_ * 128:(st_ + 1) * 128],
                                         westore[:, st_ * 16 + ue:st_ * 16 + ue + 1], ident[:])
                    mm = S_MM.inc(mi)
                    k.wait("vector", "MM", mm)
                    V.tensor_copy(werow0, PS[5][0:1, 0:512])
                    k.wait("tensor", "DV", S_DV.n)
                    mi = T.matmul(PS[4][:, 0:512], ones1x128[:], werow0, start=True, stop=True)
                    mm = S_MM.inc(mi)
                    k.wait("vector", "MM", mm)
                    V.tensor_copy(werep, PS[4][:, 0:512])

                for it in range(11):
                    if ukind == "exp":
                        srcs1 = (f_w1[ue, :, it, 0:1024], f_w1[ue, :, it, 1024:2048])
                        srcs3 = (f_w3[ue, :, it, 0:1024], f_w3[ue, :, it, 1024:2048])
                    else:
                        it22 = ue * 11 + it
                        srcs1 = (f_sw1[:, it22, 0:1024], f_sw1[:, it22, 1024:2048])
                        srcs3 = (f_sw3[:, it22, 0:1024], f_sw3[:, it22, 1024:2048])
                    sA, semA, ldA = w1r.load("sync", srcs1[0])
                    sB, semB, ldB = w1r.load("sync", srcs1[1])
                    s3A, sem3A, ld3A = w3r.load("sync", srcs3[0])
                    s3B, sem3B, ld3B = w3r.load("sync", srcs3[1])
                    k.wait("tensor", "DV", ps01_free)
                    k.wait("tensor", semA, ldA)
                    for c8 in range(8):
                        T.matmul(PS[0][:, :], w1v[sA][:, c8, :], x1h(c8), start=(c8 == 0), stop=False)
                    k.wait("tensor", semB, ldB)
                    for c8 in range(8):
                        mi = T.matmul(PS[0][:, :], w1v[sB][:, c8, :], x1h(8 + c8), start=False, stop=(c8 == 7))
                    mmw1 = S_MM.inc(mi)
                    w1r.consumed(sA, S_MM, mmw1)
                    w1r.consumed(sB, S_MM, mmw1)
                    k.wait("tensor", sem3A, ld3A)
                    for c8 in range(8):
                        T.matmul(PS[1][:, :], w3v[s3A][:, c8, :], x1h(c8), start=(c8 == 0), stop=False)
                    k.wait("tensor", sem3B, ld3B)
                    for c8 in range(8):
                        mi = T.matmul(PS[1][:, :], w3v[s3B][:, c8, :], x1h(8 + c8), start=False, stop=(c8 == 7))
                    mmw3 = S_MM.inc(mi)
                    w3r.consumed(s3A, S_MM, mmw3)
                    w3r.consumed(s3B, S_MM, mmw3)
                    k.wait("scalar", "MM", mmw3)
                    A.activation(out=stat_tmp[:, 0:512], in_=PS[0][:, :], func=AF.Silu)
                    k.wait("vector", "AC", S_AC.n)
                    V.tensor_tensor(hh[:, it, :], stat_tmp[:, 0:512], PS[1][:, :], op=ALU.mult)
                    if ukind == "exp":
                        V.tensor_tensor(hh[:, it, :], hh[:, it, :], werep, op=ALU.mult)
                    ps01_free = S_DV.n

                dv_hh = S_DV.n
                for ot in range(16):
                    if ukind == "exp":
                        src2 = (f_w2[ue, :, ot, 0:768], f_w2[ue, :, ot, 768:1408])
                    else:
                        src2 = (f_sw2[:, ot, ue * 1408:ue * 1408 + 768],
                                f_sw2[:, ot, ue * 1408 + 768:ue * 1408 + 1408])
                    s2A, sem2A, ld2A = w2r.load("sync", src2[0])
                    s2B, sem2B, ld2B = w2r.load("sync", src2[1], out_view=lambda s: w2v[s][:, 0:5, :])
                    bank = 2 + (ot % 2)
                    k.wait("tensor", "DV", max(ps_y_free[bank], dv_hh))
                    k.wait("tensor", sem2A, ld2A)
                    for i6 in range(6):
                        T.matmul(PS[bank][:, :], w2v[s2A][:, i6, :], hh[:, i6, :], start=(i6 == 0), stop=False)
                    k.wait("tensor", sem2B, ld2B)
                    for i5 in range(5):
                        mi = T.matmul(PS[bank][:, :], w2v[s2B][:, i5, :], hh[:, 6 + i5, :],
                                      start=False, stop=(i5 == 4))
                    mmw2 = S_MM.inc(mi)
                    w2r.consumed(s2A, S_MM, mmw2)
                    w2r.consumed(s2B, S_MM, mmw2)
                    k.wait("vector", "MM", mmw2)
                    if first_unit:
                        V.tensor_copy(y_t[:, ot, t0:t0 + 512], PS[bank][:, :])
                    else:
                        V.tensor_tensor(y_t[:, ot, t0:t0 + 512], y_t[:, ot, t0:t0 + 512], PS[bank][:, :], op=ALU.add)
                    ps_y_free[bank] = S_DV.n

        k.wait("vector", "MM", S_MM.n)
        for th in range(2):
            t0 = th * 512
            for c in range(NCH):
                V.tensor_tensor(arst[:, c, t0:t0 + 512], arst[:, c, t0:t0 + 512], y_t[:, c, t0:t0 + 512], op=ALU.add)
        for th in range(2):
            ln_fm(src_tiles=lambda c, th=th: arst[:, c, th * 512:(th + 1) * 512],
                  dst_tiles=lambda c, th=th: arst[:, c, th * 512:(th + 1) * 512],
                  psA=PS[0], psB=PS[1], nt=512)
        if DEBUG_TAPS:
            k.wait("gpsimd", "DV", S_DV.n)
            for c in range(NCH):
                st_tap.dma("gpsimd", taps["tap_x2"][c], arst[:, c, :], serial=True)

        k.wait("gpsimd", "DV", S_DV.n)
        for c in range(NCH):
            st_cc1.dma("gpsimd", cc1_in[c], arst[:, c, :])
        k.wait("gpsimd", st_cc1.sem, st_cc1.sem.n)
        cc1tok = S_CC.inc(nc.gpsimd.collective_compute(
            "AllGather", ALU.bypass,
            replica_groups=[[0, 1], [2, 3], [4, 5], [6, 7]],
            ins=[cc1_in[:]], outs=[cc1_out[:]]))

        st_cq = k.stream("stcq")
        st_dw = k.stream("stdw")
        ckv = AR_W[:, 0:8192].rearrange("p (kc t) -> p kc t", kc=4)
        k.wait("sync", "CC", S_CC.n)
        k.wait("sync", "MM", S_MM.n)
        k.wait("sync", "DV", S_DV.n)
        k.wait("sync", st_tap.sem, st_tap.sem.n)
        arw3_d0 = AR_RES[:].rearrange("p c t -> p (c t)")
        wti = arw3_d0[:, 0:2048].rearrange("p (c i) -> p c i", c=16)
        wti2 = arw3_d0[:, 2048:4096].rearrange("p (c i) -> p c i", c=16)
        ps_ev = 0
        wsl_free = {0: 0, 1: 0}
        for qt in range(16):
            wsl = [wti, wti2][qt % 2]
            k.wait("sync", "MM", wsl_free[qt % 2])
            if qt < 4:
                wtok = st_dw.dma("sync", wsl, d_wdkv[:, :, qt * 128:(qt + 1) * 128], serial=True)
                bias_col = bias_sb[:, qt:qt + 1]
            else:
                wtok = st_dw.dma("sync", wsl, d_wdq[:, :, (qt - 4) * 128:(qt - 3) * 128], serial=True)
                bias_col = bias_sb[:, qt:qt + 1]
            k.wait("tensor", st_dw.sem, wtok)
            for tb in range(4):
                for c in range(NCH):
                    half = (tb * 512) // TOK
                    tl = (tb * 512) % TOK
                    slot, sem, ld = xring.load("sync", cc1_out[half, c, :, tl:tl + 512])
                    k.wait("tensor", sem, ld)
                    if c == 0:
                        k.wait("tensor", "DV", ps_ev)
                    mi = T.matmul(PS[tb % 2][:, :], wsl[:, c, :], xt[slot], start=(c == 0), stop=(c == NCH - 1))
                    xring.consumed(slot, S_MM, S_MM.inc(mi))
                k.wait("vector", "MM", S_MM.n)
                if qt < 4:
                    V.tensor_scalar(ckv[:, qt, tb * 512:(tb + 1) * 512], PS[tb % 2][:, :], bias_col, None, op0=ALU.add)
                    ps_ev = S_DV.n
                else:
                    k.wait("vector", st_cq.sem, st_cq.sem.n)
                    V.tensor_scalar(stat_tmp[:, 0:512], PS[tb % 2][:, :], bias_col, None, op0=ALU.add)
                    ps_ev = S_DV.n
                    k.wait("gpsimd", "DV", ps_ev)
                    st_cq.dma("gpsimd", cq_hbm[:, qt - 4, tb * 512:(tb + 1) * 512], stat_tmp[:, 0:512], serial=True)
            wsl_free[qt % 2] = S_MM.n

        arw3 = AR_RES[:].rearrange("p c t -> p (c t)")
        kr_b = arw3[:, 0:2048]
        kR_b = arw3[:, 2048:4096]
        kC_b = arw3[:, 4096:6144]
        vtok = arw3[:, 6144:8192].rearrange("p (s i) -> p s i", s=16)
        qC_b = arw3[:, 8192:8704]
        qR_b = arw3[:, 8704:9216]
        attn_b = arw3[:, 9216:11264]
        attnT_b = arw3[:, 11264:11392]
        bvrep = arw3[:, 11392:11520]
        rotb = arw3[:, 11520:12032]
        whd = arw3[:, 12032:14080].rearrange("p (c i) -> p c i", c=16)
        k.wait("sync", st_cq.sem, st_cq.sem.n)
        ctx_all2 = ctx_all
        for h0 in range(HPC):
            k.wait("sync", "MM", S_MM.n)
            btok = st_dw.dma("sync", bvrow[:], d_buv_row[h0:h0 + 1, :], serial=True)
            k.wait("tensor", st_dw.sem, btok)
            k.wait("tensor", "DV", S_DV.n)
            k.wait("tensor", "AC", S_AC.n)
            mi = T.matmul(PS[5][:, 0:128], ones1x128[:], bvrow[:], start=True, stop=True)
            mm = S_MM.inc(mi)
            k.wait("vector", "MM", mm)
            V.tensor_copy(bvrep, PS[5][:, 0:128])

            k.wait("sync", "MM", S_MM.n)
            wtok = st_dw.dma("sync", whd, d_wkr[h0], serial=True)
            k.wait("tensor", st_dw.sem, wtok)
            for tb in range(4):
                k.wait("tensor", "DV", S_DV.n)
                for c in range(NCH):
                    half = (tb * 512) // TOK
                    tl = (tb * 512) % TOK
                    slot, sem, ld = xring.load("sync", cc1_out[half, c, :, tl:tl + 512])
                    k.wait("tensor", sem, ld)
                    mi = T.matmul(PS[0][:, :], whd[:, c, :], xt[slot], start=(c == 0), stop=(c == NCH - 1))
                    xring.consumed(slot, S_MM, S_MM.inc(mi))
                k.wait("vector", "MM", S_MM.n)
                V.tensor_scalar(kr_b[:, tb * 512:(tb + 1) * 512], PS[0][:, :],
                                bias_sb[:, 24 + h0:25 + h0], None, op0=ALU.add)
                dvv = S_DV.n
                k.wait("tensor", "DV", dvv)
                mi = T.matmul(PS[1][:, :], rotm[:], kr_b[:, tb * 512:(tb + 1) * 512], start=True, stop=True)
                mm = S_MM.inc(mi)
                k.wait("vector", "MM", mm)
                V.tensor_copy(rotb, PS[1][:, :])
                sA_, semA_, ldA_ = xring.load("sync", c_ropeA[:, tb * 512:(tb + 1) * 512])
                sB_, semB_, ldB_ = xring.load("sync", c_ropeB[:, tb * 512:(tb + 1) * 512])
                k.wait("vector", semA_, ldA_)
                k.wait("vector", semB_, ldB_)
                V.tensor_tensor(stat_tmp[:, 0:512], kr_b[:, tb * 512:(tb + 1) * 512], xt[sA_], op=ALU.mult)
                V.tensor_tensor(rotb, rotb, xt[sB_], op=ALU.mult)
                V.tensor_tensor(kR_b[:, tb * 512:(tb + 1) * 512], stat_tmp[:, 0:512], rotb, op=ALU.add)
                xring.consumed(sA_, S_DV, S_DV.n)
                xring.consumed(sB_, S_DV, S_DV.n)

            k.wait("sync", "MM", S_MM.n)
            wtok = st_dw.dma("sync", whd[:, 0:4, :], d_wuk[h0], serial=True)
            k.wait("tensor", st_dw.sem, wtok)
            for tb in range(4):
                k.wait("tensor", "DV", S_DV.n)
                for kc in range(4):
                    mi = T.matmul(PS[0][:, :], whd[:, kc, :], ckv[:, kc, tb * 512:(tb + 1) * 512],
                                  start=(kc == 0), stop=(kc == 3))
                mm = S_MM.inc(mi)
                k.wait("vector", "MM", mm)
                V.tensor_scalar(kC_b[:, tb * 512:(tb + 1) * 512], PS[0][:, :],
                                bias_sb[:, 16 + h0:17 + h0], None, op0=ALU.add)

            k.wait("sync", "MM", S_MM.n)
            wtok = st_dw.dma("sync", whd[:, 0:4, :], d_wuv[h0], serial=True)
            k.wait("tensor", st_dw.sem, wtok)
            for s_ in range(16):
                k.wait("tensor", "DV", S_DV.n)
                for kc in range(4):
                    mi = T.matmul(PS[0][:, 0:128], ckv[:, kc, s_ * 128:(s_ + 1) * 128], whd[:, kc, :],
                                  start=(kc == 0), stop=(kc == 3))
                mm = S_MM.inc(mi)
                k.wait("vector", "MM", mm)
                V.tensor_tensor(vtok[:, s_, :], PS[0][:, 0:128], bvrep, op=ALU.add)

            k.wait("sync", "MM", S_MM.n)
            wtok = st_dw.dma("sync", whd[:, 0:12, :], d_wuq[h0], serial=True)
            k.wait("tensor", st_dw.sem, wtok)
            ps_sc_free = {2: 0, 3: 0}
            ps_ctx_free = 0
            for gi in range(4):
                k.wait("tensor", "DV", S_DV.n)
                for qc in range(12):
                    slot, sem, ld = xring.load("sync", cq_hbm[:, qc, gi * 512:(gi + 1) * 512])
                    k.wait("tensor", sem, ld)
                    mi = T.matmul(PS[0][:, :], whd[:, qc, :], xt[slot], start=(qc == 0), stop=(qc == 11))
                    xring.consumed(slot, S_MM, S_MM.inc(mi))
                k.wait("vector", "MM", S_MM.n)
                V.tensor_scalar(qC_b, PS[0][:, :], bias_sb[:, 32 + h0:33 + h0], None, op0=ALU.add)
                dvq = S_DV.n
                k.wait("tensor", "DV", dvq)
                mi = T.matmul(PS[1][:, :], rotm[:], qC_b, start=True, stop=True)
                mm = S_MM.inc(mi)
                k.wait("vector", "MM", mm)
                V.tensor_copy(rotb, PS[1][:, :])
                sA_, semA_, ldA_ = xring.load("sync", c_ropeA[:, gi * 512:(gi + 1) * 512])
                sB_, semB_, ldB_ = xring.load("sync", c_ropeB[:, gi * 512:(gi + 1) * 512])
                k.wait("vector", semA_, ldA_)
                k.wait("vector", semB_, ldB_)
                V.tensor_tensor(stat_tmp[:, 0:512], qC_b, xt[sA_], op=ALU.mult)
                V.tensor_tensor(rotb, rotb, xt[sB_], op=ALU.mult)
                V.tensor_tensor(qR_b, stat_tmp[:, 0:512], rotb, op=ALU.add)
                xring.consumed(sA_, S_DV, S_DV.n)
                xring.consumed(sB_, S_DV, S_DV.n)
                dvq2 = S_DV.n

                for iq in range(4):
                    i = gi * 4 + iq
                    qoff = iq * 128
                    L = (i + 1) * 128
                    nblk_k = (i // 4) + 1
                    k.wait("tensor", "DV", dvq2)
                    for jb in range(nblk_k):
                        bank = 2 + (jb % 2)
                        k.wait("tensor", "DV", ps_sc_free[bank])
                        T.matmul(PS[bank][:, :], qC_b[:, qoff:qoff + 128], kC_b[:, jb * 512:(jb + 1) * 512],
                                 start=True, stop=False)
                        mi = T.matmul(PS[bank][:, :], qR_b[:, qoff:qoff + 128], kR_b[:, jb * 512:(jb + 1) * 512],
                                      start=False, stop=True)
                        mm = S_MM.inc(mi)
                        k.wait("vector", "MM", mm)
                        hi = min(512, L - jb * 512)
                        full_cols = min(512, (i - jb * 4) * 128)
                        if full_cols > 0:
                            V.tensor_scalar_mul(attn_b[:, jb * 512:jb * 512 + full_cols],
                                                PS[bank][:, 0:full_cols], 0.0625)
                        if full_cols < hi:
                            V.scalar_tensor_tensor(attn_b[:, jb * 512 + full_cols:jb * 512 + full_cols + 128],
                                                   PS[bank][:, full_cols:full_cols + 128], 0.0625, trimask[:],
                                                   op0=ALU.mult, op1=ALU.add)
                        ps_sc_free[bank] = S_DV.n
                    V.reduce_max(stat_tmp[:, 512:513], attn_b[:, 0:L], axis=AX.X, negate=True)
                    k.wait("scalar", "DV", S_DV.n)
                    A.activation(out=attn_b[:, 0:L], in_=attn_b[:, 0:L], func=AF.Exp,
                                 bias=stat_tmp[:, 512:513], scale=1.0, accum_out=stat_tmp[:, 513:514])
                    k.wait("vector", "AC", S_AC.n)
                    V.reciprocal(stat_tmp[:, 514:515], stat_tmp[:, 513:514])
                    V.tensor_scalar_mul(attn_b[:, 0:L], attn_b[:, 0:L], stat_tmp[:, 514:515])
                    dva = S_DV.n
                    k.wait("tensor", "DV", dva)
                    for j in range(i + 1):
                        mi = T.transpose(PS[5][:, 128:256], attn_b[:, j * 128:(j + 1) * 128], ident[:])
                        mm = S_MM.inc(mi)
                        k.wait("vector", "MM", mm)
                        k.wait("vector", "DV", S_DV.n)
                        V.tensor_copy(attnT_b, PS[5][:, 128:256])
                        dvt = S_DV.n
                        k.wait("tensor", "DV", dvt)
                        mi = T.matmul(PS[4][:, 0:128], vtok[:, j, :], attnT_b, start=(j == 0), stop=(j == i))
                    mm = S_MM.inc(mi)
                    k.wait("vector", "MM", mm)
                    V.tensor_copy(ctx_all2[:, h0, i * 128:(i + 1) * 128], PS[4][:, 0:128])
                    ps_ctx_free = S_DV.n

        k.wait("sync", "MM", S_MM.n)
        k.wait("sync", "DV", S_DV.n)
        wo_free = {0: 0, 1: 0}
        ob_free = {i: 0 for i in range(4)}
        ps_free = {0: 0, 1: 0}
        for oc in range(16):
            slot = oc % 2
            k.wait("sync", "MM", wo_free[slot])
            ldtok = st_wo[slot].dma("sync", wo_sl[slot], d_wo[:, :, oc, :])
            k.wait("tensor", st_wo[slot].sem, ldtok)
            for tb4 in range(4):
                bslot = (oc * 4 + tb4) % 2
                k.wait("tensor", "DV", ps_free[bslot])
                for h0 in range(8):
                    mi = T.matmul(PS[bslot][:, :], wo_sl[slot][:, h0, :], ctx_all2[:, h0, tb4 * 512:(tb4 + 1) * 512],
                                  start=(h0 == 0), stop=(h0 == 7))
                mm = S_MM.inc(mi)
                oslot = (oc * 4 + tb4) % 4
                k.wait("vector", "MM", mm)
                k.wait("vector", st_ob[oslot].sem, ob_free[oslot])
                V.tensor_copy(obuf[oslot], PS[bslot][:, :])
                ps_free[bslot] = S_DV.n
                k.wait("gpsimd", "DV", S_DV.n)
                t0 = tb4 * 512
                ob_free[oslot] = st_ob[oslot].dma("gpsimd", cc2_in[t0 // TOK, oc, :, t0 % TOK:t0 % TOK + 512], obuf[oslot])
            wo_free[slot] = S_MM.n
        for i in range(4):
            k.wait("gpsimd", st_ob[i].sem, st_ob[i].sem.n)
        cc2tok = S_CC.inc(nc.gpsimd.collective_compute(
            "ReduceScatter", ALU.add,
            replica_groups=[[0, 1], [2, 3], [4, 5], [6, 7]],
            ins=[cc2_in[:]], outs=[cc2_out[:]]))

        k.wait("sync", "CC", S_CC.n)
        k.wait("vector", "MM", S_MM.n)
        for i in range(4):
            k.wait("vector", st_ob[i].sem, st_ob[i].sem.n)
        k.wait("vector", st_tap.sem, st_tap.sem.n)
        for tb2 in range(2):
            for c in range(NCH):
                s0, sem0, ld0 = xring.load("sync", cc1_in[c, :, tb2 * 512:(tb2 + 1) * 512])
                s1, sem1, ld1 = xring.load("sync", cc2_out[c, :, tb2 * 512:(tb2 + 1) * 512])
                k.wait("vector", sem0, ld0)
                k.wait("vector", sem1, ld1)
                V.tensor_tensor(stat_tmp[:, 0:512], xt[s0], xt[s1], op=ALU.add)
                V.tensor_scalar(arst[:, c, tb2 * 512:(tb2 + 1) * 512], stat_tmp[:, 0:512],
                                bias_sb[:, 40 + c:41 + c], None, op0=ALU.add)
                xring.consumed(s0, S_DV, S_DV.n)
                xring.consumed(s1, S_DV, S_DV.n)
        for tb2 in range(2):
            ln_fm(src_tiles=lambda c, tb2=tb2: arst[:, c, tb2 * 512:(tb2 + 1) * 512],
                  dst_tiles=lambda c, tb2=tb2: arst[:, c, tb2 * 512:(tb2 + 1) * 512],
                  psA=PS[0], psB=PS[1], nt=512)
        if DEBUG_TAPS:
            k.wait("gpsimd", "DV", S_DV.n)
            for c in range(NCH):
                st_tap.dma("gpsimd", taps["tap_x3"][c], arst[:, c, :], serial=True)

        eA_sb = RING[:, 0:2048].rearrange("p (c i) -> p c i", c=16)
        eB_sb = RING[:, 2048:4096].rearrange("p (c i) -> p c i", c=16)
        k.wait("sync", "MM", S_MM.n)
        k.wait("sync", "DV", S_DV.n)
        ltok = st_dw.dma("sync", eA_sb, e_A[:], serial=True)
        ltok2 = st_dw.dma("sync", eB_sb, e_B[:], serial=True)
        k.wait("tensor", st_dw.sem, ltok2)
        yfin = AR_BIG[:].rearrange("p (o t) -> p o t", o=16)
        for tb2 in range(2):
            t0 = tb2 * 512
            for st_ in range(4):
                k.wait("tensor", "DV", S_DV.n)
                k.wait("tensor", "AC", S_AC.n)
                for c in range(NCH):
                    mi = T.matmul(PS[4][:, 0:8], arst[:, c, t0 + st_ * 128:t0 + st_ * 128 + 128],
                                  gwT2_sb[:, c, :], start=(c == 0), stop=(c == NCH - 1))
                mm = S_MM.inc(mi)
                k.wait("vector", "MM", mm)
                V.reduce_max(router[:, 64:65], PS[4][:, 0:8], axis=AX.X, negate=True)
                k.wait("scalar", "DV", S_DV.n)
                A.activation(out=router[:, 0:8], in_=PS[4][:, 0:8], func=AF.Exp,
                             bias=router[:, 64:65], scale=1.0, accum_out=router[:, 65:66])
                k.wait("vector", "AC", S_AC.n)
                V.reciprocal(router[:, 66:67], router[:, 65:66])
                V.tensor_scalar_mul(router[:, 0:8], router[:, 0:8], router[:, 66:67])
                V.reduce_max(router[:, 67:68], router[:, 0:8], axis=AX.X)
                V.tensor_scalar(router[:, 16:24], router[:, 0:8], router[:, 67:68], None, op0=ALU.is_ge)
                V.tensor_tensor_scan(router[:, 32:40], router[:, 16:24], router[:, 16:24], zcol[:],
                                     op0=ALU.add, op1=ALU.bypass)
                V.tensor_scalar(router[:, 48:56], router[:, 32:40], 1.0, None, op0=ALU.is_equal)
                V.tensor_tensor(router[:, 16:24], router[:, 16:24], router[:, 48:56], op=ALU.mult)
                V.scalar_tensor_tensor(router[:, 32:40], router[:, 16:24], -2.0, router[:, 0:8],
                                       op0=ALU.mult, op1=ALU.add)
                V.reduce_max(router[:, 68:69], router[:, 32:40], axis=AX.X)
                V.tensor_scalar(router[:, 48:56], router[:, 32:40], router[:, 68:69], None, op0=ALU.is_ge)
                V.tensor_tensor_scan(router[:, 32:40], router[:, 48:56], router[:, 48:56], zcol[:],
                                     op0=ALU.add, op1=ALU.bypass)
                V.tensor_scalar(router[:, 32:40], router[:, 32:40], 1.0, None, op0=ALU.is_equal)
                V.tensor_tensor(router[:, 48:56], router[:, 48:56], router[:, 32:40], op=ALU.mult)
                V.tensor_tensor(router[:, 69:70], router[:, 67:68], router[:, 68:69], op=ALU.add)
                V.reciprocal(router[:, 70:71], router[:, 69:70])
                V.tensor_tensor(router[:, 71:72], router[:, 67:68], router[:, 70:71], op=ALU.mult)
                V.tensor_tensor(router[:, 72:73], router[:, 68:69], router[:, 70:71], op=ALU.mult)
                V.tensor_scalar(router[:, 16:24], router[:, 16:24], router[:, 71:72], None, op0=ALU.mult)
                V.tensor_scalar(router[:, 48:56], router[:, 48:56], router[:, 72:73], None, op0=ALU.mult)
                V.tensor_tensor(westore[:, st_ * 16:st_ * 16 + 8], router[:, 16:24], router[:, 48:56], op=ALU.add)
            dv_we = S_DV.n
            k.wait("tensor", "DV", dv_we)
            for st_ in range(4):
                mi = T.transpose(PS[5][0:8, st_ * 128:(st_ + 1) * 128], westore[:, st_ * 16:st_ * 16 + 8], ident[:])
            mm = S_MM.inc(mi)
            k.wait("vector", "MM", mm)
            V.tensor_copy(stat_tmp[0:8, 512:1024], PS[5][0:8, 0:512])
            dvc = S_DV.n
            k.wait("tensor", "DV", dvc)
            mi = T.matmul(PS[4][:, 0:512], e16_sb[:], stat_tmp[0:8, 512:1024], start=True, stop=True)
            mmc = S_MM.inc(mi)
            for c in range(NCH):
                mi = T.matmul(PS[0][:, :], eA_sb[:, c, :], arst[:, c, t0:t0 + 512], start=(c == 0), stop=(c == NCH - 1))
            mm = S_MM.inc(mi)
            k.wait("vector", "MM", mm)
            V.tensor_copy(stat_tmp[:, 0:512], PS[0][:, :])
            V.tensor_tensor(stat_tmp[:, 0:512], stat_tmp[:, 0:512], PS[4][:, 0:512], op=ALU.mult)
            dvg = S_DV.n
            k.wait("tensor", "DV", dvg)
            ps_e_free = {2: 0, 3: 0}
            for ot in range(16):
                bank = 2 + (ot % 2)
                k.wait("tensor", "DV", ps_e_free[bank])
                mi = T.matmul(PS[bank][:, :], eB_sb[:, ot, :], stat_tmp[:, 0:512], start=True, stop=True)
                mm = S_MM.inc(mi)
                k.wait("vector", "MM", mm)
                V.scalar_tensor_tensor(yfin[:, ot, t0:t0 + 512], PS[bank][:, :], 2.0,
                                       arst[:, ot, t0:t0 + 512], op0=ALU.mult, op1=ALU.add)
                ps_e_free[bank] = S_DV.n
        for tb2 in range(2):
            ln_fm(src_tiles=lambda c, tb2=tb2: yfin[:, c, tb2 * 512:(tb2 + 1) * 512],
                  dst_tiles=lambda c, tb2=tb2: yfin[:, c, tb2 * 512:(tb2 + 1) * 512],
                  psA=PS[0], psB=PS[1], nt=512)
        k.wait("gpsimd", "DV", S_DV.n)
        for c in range(NCH):
            st_y.dma("gpsimd", y[c], yfin[:, c, :])

        k.fence_all()
        k.wait("gpsimd", st_y.sem, st_y.sem.n)
        k.wait("gpsimd", st_tap.sem, st_tap.sem.n)
        k.wait("gpsimd", "CC", S_CC.n)

    return nc


_CACHED = {}


def kernel(**inputs):
    in_maps = prep_inputs(inputs)
    key = DEBUG_TAPS
    if key not in _CACHED:
        _CACHED[key] = build_program()
    nc = _CACHED[key]
    res = run_bass_kernel_spmd(nc, in_maps, core_ids=list(range(NC)))
    out = np.empty((B, S, D), dtype=np.float32)
    for c in range(NC):
        b, g = c // 2, c % 2
        yv = res.results[c]["y"]
        out[b, g * TOK:(g + 1) * TOK] = yv.transpose(2, 0, 1).reshape(TOK, D)
    kernel.last_results = res
    return out


# version 04 (final)
# speedup vs baseline: 1.0479x; 1.0479x over previous
import numpy as np
import concourse.bass as bass
import concourse.mybir as mybir
from concourse.bass_utils import run_bass_kernel_spmd

F32 = mybir.dt.float32
AF = mybir.ActivationFunctionType
ALU = mybir.AluOpType
AX = mybir.AxisListType

B, S, D = 4, 2048, 2048
NC = 8
H, HD = 16, 128
HPC = 8
TOK = 1024
NCH = 16
NBLK = 4
EPS_LN = 1e-5
EPS_L0 = 1e-6
FMIN = float(np.finfo(np.float32).min)
SCALE_L0 = float(HD ** (-0.5))
ROPE_BASE = 10000.0

DEBUG_TAPS = False
SPARSE_MOE = False
SKIP_EXPERTS = 0
CAP = 256



def _split_perm():
    return np.concatenate([np.arange(0, 128, 2), np.arange(1, 128, 2)])


def _perm_cols_per_head(w, n_heads=16):
    p = _split_perm()
    w = w.reshape(w.shape[0], n_heads, 128)[:, :, p]
    return np.ascontiguousarray(w.reshape(w.shape[0], -1))


def _rope_tables():
    half = 64
    pos = np.arange(S, dtype=np.float64)
    inv = 1.0 / (ROPE_BASE ** (np.arange(half, dtype=np.float64) * 2.0 / 128.0))
    ang = pos[:, None] * inv[None, :]
    cosT = np.cos(ang).T.astype(np.float32)
    sinT = np.sin(ang).T.astype(np.float32)
    A = np.empty((128, S), dtype=np.float32)
    A[0:64] = cosT
    A[64:128] = cosT
    Bt = np.empty((128, S), dtype=np.float32)
    Bt[0:64] = sinT
    Bt[64:128] = sinT
    return A, Bt


def _rot_matrix():
    P = np.zeros((128, 128), dtype=np.float32)
    for j in range(64):
        P[64 + j, j] = -1.0
        P[j, 64 + j] = 1.0
    return P


def prep_inputs(inp):
    f32 = lambda a: np.ascontiguousarray(a, dtype=np.float32)
    x = np.asarray(inp["L0_attn_hidden_states"], dtype=np.float32)

    shared = {}
    shared["c_ident"] = f32(np.eye(128))
    tri = np.zeros((128, 128), dtype=np.float32)
    tri[np.triu_indices(128, k=1)] = FMIN
    shared["c_trimask"] = tri
    shared["c_keepT"] = f32(np.triu(np.ones((128, 128))))
    ra, rb = _rope_tables()
    shared["c_ropeA"] = ra
    shared["c_ropeB"] = rb
    shared["c_rot"] = _rot_matrix()
    shared["c_ones128"] = f32(np.ones((128, 1)))
    shared["c_ones1x128"] = f32(np.ones((1, 128)))
    shared["c_lstrict"] = f32(np.triu(np.ones((128, 128)), 1))
    shared["c_eoff"] = f32((np.arange(16) * CAP)[None, :])
    shared["c_tok16"] = np.ascontiguousarray(
        (np.arange(8)[None, :] * 128 + np.arange(128)[:, None]).astype(np.int16))

    wq_t = np.asarray(inp["L0_attn_w_q"], dtype=np.float32).T
    wk_t = np.asarray(inp["L0_attn_w_k"], dtype=np.float32).T
    wv_t = np.asarray(inp["L0_attn_w_v"], dtype=np.float32).T
    wo_t = np.asarray(inp["L0_attn_w_o"], dtype=np.float32).T
    mem = np.asarray(inp["L0_attn_memories"], dtype=np.float32)
    mn = np.asarray(inp["L0_attn_memory_norms"], dtype=np.float32)
    gate = np.asarray(inp["L0_attn_gate"], dtype=np.float32)
    shared["a_mn_all"] = f32(mn.transpose(2, 0, 1).reshape(128, 64))

    def head_tiles(wT, gh):
        sl = wT[:, gh * 128:(gh + 1) * 128]
        return f32(sl.reshape(16, 128, 128).transpose(1, 0, 2))

    gw = np.asarray(inp["L0_ffn_gate_w"], dtype=np.float32)
    shared["f_gwT"] = f32(gw.T.reshape(16, 128, 16).transpose(1, 0, 2))
    w1 = np.asarray(inp["L0_ffn_w1"], dtype=np.float32)
    w3 = np.asarray(inp["L0_ffn_w3"], dtype=np.float32)
    w2 = np.asarray(inp["L0_ffn_w2"], dtype=np.float32)

    def expT(w):
        E = w.shape[0]
        a = w.transpose(0, 2, 1).reshape(E, 16, 128, 11, 128)
        a = a.transpose(0, 2, 3, 1, 4).reshape(E, 128, 11, 2048)
        return np.ascontiguousarray(a, dtype=np.float32)

    shared["f_w1"] = expT(w1)
    shared["f_w3"] = expT(w3)
    a = w2.transpose(0, 2, 1).reshape(16, 11, 128, 16, 128)
    a = a.transpose(0, 2, 3, 1, 4).reshape(16, 128, 16, 1408)
    shared["f_w2"] = np.ascontiguousarray(a, dtype=np.float32)
    sw1 = np.asarray(inp["L0_ffn_sw1"], dtype=np.float32)
    sw3 = np.asarray(inp["L0_ffn_sw3"], dtype=np.float32)
    sw2 = np.asarray(inp["L0_ffn_sw2"], dtype=np.float32)
    a = sw1.T.reshape(16, 128, 22, 128).transpose(1, 2, 0, 3).reshape(128, 22, 2048)
    shared["f_sw1"] = np.ascontiguousarray(a, dtype=np.float32)
    a = sw3.T.reshape(16, 128, 22, 128).transpose(1, 2, 0, 3).reshape(128, 22, 2048)
    shared["f_sw3"] = np.ascontiguousarray(a, dtype=np.float32)
    a = sw2.T.reshape(22, 128, 16, 128).transpose(1, 2, 0, 3).reshape(128, 16, 2816)
    shared["f_sw2"] = np.ascontiguousarray(a, dtype=np.float32)

    wdkv = np.asarray(inp["L1_attn_W_DKV"], dtype=np.float32)
    shared["d_wdkv"] = f32(wdkv.reshape(16, 128, 512).transpose(1, 0, 2))
    shared["d_bdkv"] = f32(np.asarray(inp["L1_attn_b_DKV"], dtype=np.float32).reshape(4, 128).T)
    wuk = _perm_cols_per_head(np.asarray(inp["L1_attn_W_UK"], dtype=np.float32))
    wuv = np.asarray(inp["L1_attn_W_UV"], dtype=np.float32)
    buk = _perm_cols_per_head(np.asarray(inp["L1_attn_b_UK"], dtype=np.float32)[None, :])[0]
    buv = np.asarray(inp["L1_attn_b_UV"], dtype=np.float32)
    wdq = np.asarray(inp["L1_attn_W_DQ"], dtype=np.float32)
    shared["d_wdq"] = f32(wdq.reshape(16, 128, 1536).transpose(1, 0, 2))
    shared["d_bdq"] = f32(np.asarray(inp["L1_attn_b_DQ"], dtype=np.float32).reshape(12, 128).T)
    wuq = _perm_cols_per_head(np.asarray(inp["L1_attn_W_UQ"], dtype=np.float32))
    buq = _perm_cols_per_head(np.asarray(inp["L1_attn_b_UQ"], dtype=np.float32)[None, :])[0]
    wkr = _perm_cols_per_head(np.asarray(inp["L1_attn_W_KR"], dtype=np.float32))
    bkr = _perm_cols_per_head(np.asarray(inp["L1_attn_b_KR"], dtype=np.float32)[None, :])[0]
    wo1 = np.asarray(inp["L1_attn_W_O"], dtype=np.float32)
    shared["d_bo"] = f32(np.asarray(inp["L1_attn_b_O"], dtype=np.float32).reshape(16, 128).T)

    gw1 = np.asarray(inp["L1_ffn_gate_w"], dtype=np.float32)
    shared["e_gwT"] = f32(gw1.T.reshape(16, 128, 8).transpose(1, 0, 2))
    la = np.asarray(inp["L1_ffn_lora_A"], dtype=np.float32)
    Ast = la.reshape(128, 2048).T
    shared["e_A"] = f32(Ast.reshape(16, 128, 128).transpose(1, 0, 2))
    lb = np.asarray(inp["L1_ffn_lora_B"], dtype=np.float32)
    Bst = lb.transpose(0, 2, 1).reshape(128, 2048)
    shared["e_B"] = f32(Bst.reshape(128, 16, 128))
    e16 = np.zeros((8, 128), dtype=np.float32)
    for e in range(8):
        e16[e, e * 16:(e + 1) * 16] = 1.0
    shared["e_E16"] = e16

    in_maps = []
    for c in range(NC):
        b, g = c // 2, c % 2
        m = dict(shared)
        xb = x[b]
        m["x_fm"] = f32(xb.T.reshape(16, 128, 2048).transpose(1, 0, 2))
        m["x_own"] = f32(xb[g * TOK:(g + 1) * TOK].T.reshape(16, 128, TOK).transpose(1, 0, 2))
        heads = list(range(8 * g, 8 * g + 8))
        m["a_wq"] = np.stack([head_tiles(wq_t, gh) for gh in heads])
        m["a_wk"] = np.stack([head_tiles(wk_t, gh) for gh in heads])
        m["a_wv"] = np.stack([head_tiles(wv_t, gh) for gh in heads])
        me = np.zeros((8, 128, 4, 129), dtype=np.float32)
        for i, gh in enumerate(heads):
            for mm_ in range(4):
                me[i, :, mm_, :128] = mem[mm_, gh]
                me[i, :, mm_, 128] = mn[mm_, gh]
        m["a_mem"] = me
        m["a_gate8"] = f32(gate[8 * g:8 * g + 8][None, :])
        wo_rows = wo_t[g * 1024:(g + 1) * 1024]
        m["a_wo"] = np.ascontiguousarray(wo_rows.reshape(8, 128, 16, 128).transpose(1, 0, 2, 3), dtype=np.float32)
        m["d_wuk"] = np.stack([f32(wuk[:, gh * 128:(gh + 1) * 128].reshape(4, 128, 128).transpose(1, 0, 2)) for gh in heads])
        m["d_wuv"] = np.stack([f32(wuv[:, gh * 128:(gh + 1) * 128].reshape(4, 128, 128).transpose(1, 0, 2)) for gh in heads])
        m["d_buk"] = f32(np.stack([buk[gh * 128:(gh + 1) * 128] for gh in heads], axis=1))
        m["d_buv_row"] = f32(np.stack([buv[gh * 128:(gh + 1) * 128] for gh in heads], axis=0))
        m["d_wuq"] = np.stack([f32(wuq[:, gh * 128:(gh + 1) * 128].reshape(12, 128, 128).transpose(1, 0, 2)) for gh in heads])
        m["d_buq"] = f32(np.stack([buq[gh * 128:(gh + 1) * 128] for gh in heads], axis=1))
        m["d_wkr"] = np.stack([f32(wkr[:, gh * 128:(gh + 1) * 128].reshape(16, 128, 128).transpose(1, 0, 2)) for gh in heads])
        m["d_bkr"] = f32(np.stack([bkr[gh * 128:(gh + 1) * 128] for gh in heads], axis=1))
        wo1_rows = wo1[g * 1024:(g + 1) * 1024]
        m["d_wo"] = np.ascontiguousarray(wo1_rows.reshape(8, 128, 16, 128).transpose(1, 0, 2, 3), dtype=np.float32)
        in_maps.append(m)
    return in_maps



class Sem:
    def __init__(self, ctx, nc, name):
        self.name = name
        self.h = ctx.enter_context(nc.semaphore(name))
        self.n = 0

    def inc(self, instr, k=1):
        instr.then_inc(self.h, k)
        self.n += k
        return self.n


class Stream:

    def __init__(self, k, name):
        self.k = k
        self.sem = Sem(k.ctx, k.nc, name)

    def dma(self, queue, out, in_, serial=False, **kw):
        if serial and self.sem.n > 0:
            self.k.wait(queue, self.sem, self.sem.n)
        getattr(self.k.nc, queue).dma_start(out=out, in_=in_, **kw).then_inc(self.sem.h, 16)
        self.sem.n += 16
        return self.sem.n


class Ring:

    def __init__(self, k, name, tiles):
        self.k = k
        self.tiles = tiles
        self.streams = [k.stream(f"{name}{i}") for i in range(len(tiles))]
        self.guard = [None] * len(tiles)
        self.i = 0

    def load(self, queue, src, out_view=None):
        s = self.i % len(self.tiles)
        self.i += 1
        if self.guard[s] is not None:
            self.k.wait(queue, self.guard[s][0], self.guard[s][1])
        dst = self.tiles[s] if out_view is None else out_view(s)
        tok = self.streams[s].dma(queue, dst, src)
        return s, self.streams[s].sem, tok

    def consumed(self, s, sem, val):
        self.guard[s] = (sem, val)


class SerialEngine:

    def __init__(self, k, eng, sem):
        self._k = k
        self._eng = eng
        self._sem = sem

    def __getattr__(self, attr):
        k = self._k
        f = getattr(getattr(k.nc, self._eng), attr)

        def wrapped(*a, **kw):
            k.wait(self._eng, self._sem, k.sems[self._sem].n)
            instr = f(*a, **kw)
            k.sems[self._sem].inc(instr)
            return instr

        return wrapped


class K:
    def __init__(self, ctx, nc):
        self.ctx = ctx
        self.nc = nc
        self.sems = {}
        for name in ["MM", "DV", "AC", "CC"]:
            self.sems[name] = Sem(ctx, nc, "s_" + name)
        self.all_sems = list(self.sems.values())
        self.waited = {e: {} for e in ["tensor", "vector", "scalar", "gpsimd", "sync"]}

    def new_sem(self, name):
        s = Sem(self.ctx, self.nc, name)
        self.all_sems.append(s)
        return s

    def stream(self, name):
        st = Stream(self, name)
        self.all_sems.append(st.sem)
        return st

    def _sem(self, s):
        return self.sems[s] if isinstance(s, str) else s

    def wait(self, eng_name, sem, val):
        if val <= 0:
            return
        sem = self._sem(sem)
        w = self.waited[eng_name]
        if w.get(sem.name, 0) >= val:
            return
        getattr(self.nc, eng_name).wait_ge(sem.h, val)
        w[sem.name] = val

    def fence_all(self):
        for e in self.waited:
            for s in self.all_sems:
                self.wait(e, s, s.n)


def build_program():
    import contextlib
    ctx = contextlib.ExitStack()
    nc = bass.Bass("TRN2", target_bir_lowering=False, debug=False)

    def din(name, shape, dt=F32):
        return nc.dram_tensor(name, list(shape), dt, kind="ExternalInput").ap()

    def dout(name, shape):
        return nc.dram_tensor(name, list(shape), F32, kind="ExternalOutput").ap()

    def dint(name, shape, dt=F32):
        return nc.dram_tensor(name, list(shape), dt, kind="Internal").ap()

    x_fm = din("x_fm", (128, 16, 2048))
    x_own = din("x_own", (128, 16, TOK))
    c_ident = din("c_ident", (128, 128))
    c_trimask = din("c_trimask", (128, 128))
    c_keepT = din("c_keepT", (128, 128))
    c_ropeA = din("c_ropeA", (128, 2048))
    c_ropeB = din("c_ropeB", (128, 2048))
    c_rot = din("c_rot", (128, 128))
    c_lstrict = din("c_lstrict", (128, 128))
    c_eoff = din("c_eoff", (1, 16))
    c_tok16 = din("c_tok16", (128, 8), mybir.dt.int16)
    c_ones128 = din("c_ones128", (128, 1))
    c_ones1x128 = din("c_ones1x128", (1, 128))
    a_wq = din("a_wq", (8, 128, 16, 128))
    a_wk = din("a_wk", (8, 128, 16, 128))
    a_wv = din("a_wv", (8, 128, 16, 128))
    a_mem = din("a_mem", (8, 128, 4, 129))
    a_mn_all = din("a_mn_all", (128, 64))
    a_gate8 = din("a_gate8", (1, 8))
    a_wo = din("a_wo", (128, 8, 16, 128))
    f_gwT = din("f_gwT", (128, 16, 16))
    f_w1 = din("f_w1", (16, 128, 11, 2048))
    f_w3 = din("f_w3", (16, 128, 11, 2048))
    f_w2 = din("f_w2", (16, 128, 16, 1408))
    f_sw1 = din("f_sw1", (128, 22, 2048))
    f_sw3 = din("f_sw3", (128, 22, 2048))
    f_sw2 = din("f_sw2", (128, 16, 2816))
    d_wdkv = din("d_wdkv", (128, 16, 512))
    d_bdkv = din("d_bdkv", (128, 4))
    d_wdq = din("d_wdq", (128, 16, 1536))
    d_bdq = din("d_bdq", (128, 12))
    d_wuk = din("d_wuk", (8, 128, 4, 128))
    d_wuv = din("d_wuv", (8, 128, 4, 128))
    d_buk = din("d_buk", (128, 8))
    d_buv_row = din("d_buv_row", (8, 128))
    d_wuq = din("d_wuq", (8, 128, 12, 128))
    d_buq = din("d_buq", (128, 8))
    d_wkr = din("d_wkr", (8, 128, 16, 128))
    d_bkr = din("d_bkr", (128, 8))
    d_wo = din("d_wo", (128, 8, 16, 128))
    d_bo = din("d_bo", (128, 16))
    e_gwT = din("e_gwT", (128, 16, 8))
    e_A = din("e_A", (128, 16, 128))
    e_B = din("e_B", (128, 16, 128))
    e_E16 = din("e_E16", (8, 128))

    y = dout("y", (16, 128, TOK))

    cc0_in = dint("cc0_in", (2, 16, 128, TOK))
    cc0_out = dint("cc0_out", (16, 128, TOK))
    cc1_in = dint("cc1_in", (16, 128, TOK))
    cc1_out = dint("cc1_out", (2, 16, 128, TOK))
    cc2_in = dint("cc2_in", (2, 16, 128, TOK))
    cc2_out = dint("cc2_out", (16, 128, TOK))
    cq_hbm = dint("cq_hbm", (128, 12, 2048))
    x1tm = dint("x1tm", (1025, 2048))
    ymoe = dint("ymoe", (1025, 2048))
    perm_hbm = dint("perm_hbm", (4096, 1), mybir.dt.int16)
    wslot_hbm = dint("wslot_hbm", (4096, 1))

    taps = {}
    if DEBUG_TAPS:
        taps["tap_ctx0"] = dout("tap_ctx0", (128, 8, 2048))
        taps["tap_a0"] = dout("tap_a0", (16, 128, TOK))
        taps["tap_x1"] = dout("tap_x1", (16, 128, TOK))
        taps["tap_x2"] = dout("tap_x2", (16, 128, TOK))
        taps["tap_x3"] = dout("tap_x3", (16, 128, TOK))

    with ctx:
        sb = lambda name, shape: ctx.enter_context(nc.sbuf_tensor(name, list(shape), F32))
        pst = lambda name, shape: ctx.enter_context(nc.psum_tensor(name, list(shape), F32))

        k = K(ctx, nc)
        S_MM, S_DV, S_AC, S_CC = (k.sems[n] for n in ["MM", "DV", "AC", "CC"])
        S_SW = k.new_sem("s_SW")
        if SPARSE_MOE:
            from concourse import library_config
            nc.gpsimd.load_library(library_config.mlp)
        V = SerialEngine(k, "vector", "DV")
        A = SerialEngine(k, "scalar", "AC")
        T = nc.tensor

        AR_RES = sb("AR_RES", (128, 16, TOK))
        AR_BIG = sb("AR_BIG", (128, 16384))
        AR_W = sb("AR_W", (128, 8192))
        RING = sb("RING", (128, 5632))
        ident = sb("ident", (128, 128))
        trimask = sb("trimask", (128, 128))
        keepT = sb("keepT", (128, 128))
        rotm = sb("rotm", (128, 128))
        bias_sb = sb("bias_sb", (128, 64))
        bvrow = sb("bvrow", (1, 128))
        gwT2_sb = sb("gwT2_sb", (128, 16, 8))
        e16_sb = sb("e16_sb", (8, 128))
        lstrict = sb("lstrict", (128, 128))
        eoffrow = sb("eoffrow", (1, 16))
        eoffrep = sb("eoffrep", (128, 16))
        m1sb = sb("m1sb", (128, 128))
        m2sb = sb("m2sb", (128, 128))
        wv_sb = sb("wv_sb", (128, 16))
        baserow = sb("baserow", (1, 16))
        sltmp = sb("sltmp", (128, 64))
        tok16_sb = ctx.enter_context(nc.sbuf_tensor("tok16_sb", [128, 8], mybir.dt.int16))
        idx16 = ctx.enter_context(nc.sbuf_tensor("idx16", [128, 16], mybir.dt.int16))
        pinit = ctx.enter_context(nc.sbuf_tensor("pinit", [128, 32], mybir.dt.int16))
        slot32 = ctx.enter_context(nc.sbuf_tensor("slot32", [128, 2], mybir.dt.uint32))
        ones128 = sb("ones128", (128, 1))
        ones1x128 = sb("ones1x128", (1, 128))
        rowbuf = sb("rowbuf", (1, 1664))
        stat_tmp = sb("stat_tmp", (128, 1024))
        xbar = sb("xbar", (128, 16))
        xbar_t = sb("xbar_t", (128, 16))
        gwT_sb = sb("gwT_sb", (128, 16, 16))
        router = sb("router", (128, 96))
        westore = sb("westore", (128, 64))
        zcol = sb("zcol", (128, 1))
        small = sb("small", (1, 64))
        gate_sb = sb("gate_sb", (1, 8))
        sig8 = sb("sig8", (1, 8))
        mn_all = sb("mn_all", (128, 64))
        emask = sb("emask", (1, 8))
        cgrep = sb("cgrep", (128, 8))

        xt = [RING[:, i * 512:(i + 1) * 512] for i in range(11)]

        PS = [pst(f"ps{i}", (128, 512)) for i in range(8)]

        def ln_fm(src_tiles, dst_tiles, psA, psB, nt):
            k.wait("tensor", "DV", S_DV.n)
            k.wait("tensor", "AC", S_AC.n)
            k.wait("scalar", "DV", S_DV.n)
            mm_tok = {}
            for c in range(NCH):
                slot = c % 2
                if c >= 2:
                    k.wait("scalar", "MM", mm_tok[slot])
                A.activation(out=stat_tmp[:, slot * nt:(slot + 1) * nt], in_=src_tiles(c), func=AF.Square)
                ac = S_AC.n
                T.matmul(psA[0:1, 0:nt], ones128[:], src_tiles(c), start=(c == 0), stop=(c == NCH - 1))
                k.wait("tensor", "AC", ac)
                mm = S_MM.inc(T.matmul(psB[0:1, 0:nt], ones128[:], stat_tmp[:, slot * nt:(slot + 1) * nt],
                                       start=(c == 0), stop=(c == NCH - 1)))
                mm_tok[slot] = mm
            k.wait("vector", "MM", S_MM.n)
            mean = rowbuf[0:1, 0:nt]
            var = rowbuf[0:1, nt:2 * nt]
            rstd = rowbuf[0:1, 2 * nt:3 * nt]
            V.tensor_scalar_mul(mean, psA[0:1, 0:nt], 1.0 / D)
            V.tensor_scalar_mul(var, psB[0:1, 0:nt], 1.0 / D)
            V.tensor_tensor(rstd, mean, mean, op=ALU.mult)
            V.tensor_tensor(var, var, rstd, op=ALU.subtract)
            V.tensor_scalar_add(var, var, EPS_LN)
            dv = S_DV.n
            k.wait("scalar", "DV", dv)
            A.activation(out=var, in_=var, func=AF.Sqrt)
            k.wait("vector", "AC", S_AC.n)
            V.reciprocal(rstd, var)
            V.tensor_tensor(mean, mean, rstd, op=ALU.mult)
            V.tensor_scalar_mul(mean, mean, -1.0)
            k.wait("tensor", "DV", S_DV.n)
            S_MM.inc(T.matmul(psA[:, 0:nt], ones1x128[:], rstd, start=True, stop=True))
            mmB = S_MM.inc(T.matmul(psB[:, 0:nt], ones1x128[:], mean, start=True, stop=True))
            k.wait("vector", "MM", mmB)
            for c in range(NCH):
                V.tensor_tensor(stat_tmp[:, 0:nt], src_tiles(c), psA[:, 0:nt], op=ALU.mult)
                V.tensor_tensor(dst_tiles(c), stat_tmp[:, 0:nt], psB[:, 0:nt], op=ALU.add)

        st_c = k.stream("stc")
        st_wq = k.stream("stwq")
        st_wk = k.stream("stwk")
        st_wv = k.stream("stwv")
        st_mem = k.stream("stmem")
        st_wo = [k.stream("stwo0"), k.stream("stwo1")]
        st_ob = [k.stream(f"stob{i}") for i in range(4)]
        st_tap = k.stream("sttap")
        st_y = k.stream("sty")
        st_cc1 = k.stream("stcc1")
        st_dw = k.stream("stdw")
        xring = Ring(k, "xr", xt)

        st_c.dma("sync", ident[:], c_ident[:])
        st_c.dma("sync", trimask[:], c_trimask[:])
        st_c.dma("sync", keepT[:], c_keepT[:])
        st_c.dma("sync", rotm[:], c_rot[:])
        st_c.dma("sync", bias_sb[:, 0:4], d_bdkv[:])
        st_c.dma("sync", bias_sb[:, 4:16], d_bdq[:])
        st_c.dma("sync", bias_sb[:, 16:24], d_buk[:])
        st_c.dma("sync", bias_sb[:, 24:32], d_bkr[:])
        st_c.dma("sync", bias_sb[:, 32:40], d_buq[:])
        st_c.dma("sync", bias_sb[:, 40:56], d_bo[:])
        st_c.dma("sync", gwT2_sb[:], e_gwT[:])
        st_c.dma("sync", e16_sb[:], e_E16[:])
        st_c.dma("sync", lstrict[:], c_lstrict[:])
        st_c.dma("sync", eoffrow[:], c_eoff[:])
        st_c.dma("sync", tok16_sb[:], c_tok16[:])
        st_c.dma("sync", ones128[:], c_ones128[:])
        st_c.dma("sync", ones1x128[:], c_ones1x128[:])
        st_c.dma("sync", mn_all[:], a_mn_all[:])
        st_c.dma("sync", gwT_sb[:], f_gwT[:])
        const_tok = st_c.dma("sync", gate_sb[:], a_gate8[:])

        k.wait("scalar", st_c.sem, const_tok)
        A.activation(out=sig8[:], in_=gate_sb[:], func=AF.Sigmoid)
        sig_ac = S_AC.n
        V.memset(zcol[:], 0.0)
        k.wait("tensor", st_c.sem, const_tok)
        mm = S_MM.inc(T.matmul(PS[0][0:1, 0:64], ones128[:], mn_all[:], start=True, stop=True))
        k.wait("vector", "MM", mm)
        for m_ in range(4):
            V.reduce_sum(small[0:1, m_:m_ + 1], PS[0][0:1, m_ * 16:(m_ + 1) * 16], axis=AX.X)
        V.tensor_scalar(emask[0:1, 0:4], small[0:1, 0:4], 1e-6, -1e38, op0=ALU.is_lt, op1=ALU.mult)
        V.tensor_scalar(small[0:1, 4:8], small[0:1, 0:4], 1e-6, None, op0=ALU.is_lt)
        V.tensor_reduce(small[0:1, 12:13], small[0:1, 4:8], axis=AX.X, op=ALU.min)
        V.tensor_scalar(small[0:1, 13:14], small[0:1, 12:13], -1.0, 1.0, op0=ALU.mult, op1=ALU.add)

        for blk in range(4):
            for c in range(NCH):
                slot, sem, ld = xring.load("sync", x_fm[:, c, blk * 512:(blk + 1) * 512])
                k.wait("vector", sem, ld)
                V.reduce_sum(xbar_t[:, c:c + 1], xt[slot], axis=AX.X)
                xring.consumed(slot, S_DV, S_DV.n)
            if blk == 0:
                V.tensor_copy(xbar[:], xbar_t[:])
            else:
                V.tensor_tensor(xbar[:], xbar[:], xbar_t[:], op=ALU.add)
        a0_done_dv = S_DV.n

        arw = AR_RES[:].rearrange("p c t -> p (c t)")
        wq_t_ = arw[:, 0:2048].rearrange("p (c i) -> p c i", c=16)
        wk_t_ = arw[:, 2048:4096].rearrange("p (c i) -> p c i", c=16)
        wv_t_ = arw[:, 4096:6144].rearrange("p (c i) -> p c i", c=16)
        mem_t = arw[:, 6144:6660].rearrange("p (m i) -> p m i", m=4)
        qp_blk = arw[:, 6660:7172]
        kp_blk = arw[:, 7172:7684]
        vext_blk = arw[:, 7684:8200].rearrange("p (s i) -> p s i", s=4)
        kptok = arw[:, 8200:8328]
        sT_t = arw[:, 8328:8456]
        sbS = arw[:, 8456:8585]
        macc = arw[:, 8585:8713]
        mtmp = arw[:, 8713:8841]
        lout = arw[:, 8841:8969]
        qbar_s = arw[:, 8969:8970]
        ctx_all = AR_BIG[:].rearrange("p (h t) -> p h t", h=8)

        k.wait("sync", "DV", a0_done_dv)
        for h0 in range(HPC):
            tq = st_wq.dma("sync", wq_t_, a_wq[h0])
            tk = st_wk.dma("sync", wk_t_, a_wk[h0])
            tv = st_wv.dma("sync", wv_t_, a_wv[h0])
            tm = st_mem.dma("sync", mem_t, a_mem[h0])

            k.wait("tensor", st_wq.sem, tq)
            k.wait("tensor", st_wk.sem, tk)
            k.wait("tensor", st_wv.sem, tv)
            k.wait("tensor", st_mem.sem, tm)
            k.wait("tensor", "DV", a0_done_dv)
            for c in range(NCH):
                mi = T.matmul(PS[6][0:128, 0:1], wq_t_[:, c, :], xbar[:, c:c + 1],
                              start=(c == 0), stop=(c == NCH - 1))
            mm = S_MM.inc(mi)
            k.wait("vector", "MM", mm)
            V.tensor_copy(qbar_s, PS[6][0:128, 0:1])
            k.wait("tensor", "DV", S_DV.n)
            mm = S_MM.inc(T.matmul(PS[6][0:1, 0:4], qbar_s, mem_t[:, :, 128], start=True, stop=True))
            k.wait("vector", "MM", mm)
            r0 = rowbuf[0:1, 0:4]
            V.tensor_scalar_mul(r0, PS[6][0:1, 0:4], SCALE_L0 / S)
            V.tensor_tensor(r0, r0, emask[0:1, 0:4], op=ALU.add)
            V.reduce_max(rowbuf[0:1, 4:5], r0, axis=AX.X, negate=True)
            k.wait("scalar", "DV", S_DV.n)
            A.activation(out=r0, in_=r0, func=AF.Exp, bias=rowbuf[0:1, 4:5], scale=1.0,
                         accum_out=rowbuf[0:1, 5:6])
            k.wait("vector", "AC", S_AC.n)
            V.reciprocal(rowbuf[0:1, 6:7], rowbuf[0:1, 5:6])
            V.tensor_scalar(r0, r0, rowbuf[0:1, 6:7], small[0:1, 13:14], op0=ALU.mult, op1=ALU.mult)
            V.tensor_scalar(rowbuf[0:1, 8:12], r0, sig8[0:1, h0:h0 + 1], None, op0=ALU.mult)
            V.tensor_scalar(rowbuf[0:1, 7:8], sig8[0:1, h0:h0 + 1], -1.0, 1.0, op0=ALU.mult, op1=ALU.add)
            k.wait("tensor", "DV", S_DV.n)
            mm = S_MM.inc(T.matmul(PS[6][:, 0:8], ones1x128[:], rowbuf[0:1, 4:12], start=True, stop=True))
            k.wait("vector", "MM", mm)
            V.tensor_copy(cgrep[:, 0:8], PS[6][:, 0:8])
            cg_col = lambda m_: cgrep[:, 4 + m_:5 + m_]
            g1m_col = cgrep[:, 3:4]

            V.memset(sbS[:, :], 0.0)
            V.memset(vext_blk[:, :, 128], 1.0)

            prev_evac_dv = S_DV.n
            for blk in range(NBLK):
                tb = blk * 512
                k.wait("tensor", "DV", prev_evac_dv)
                for c in range(NCH):
                    slot, sem, ld = xring.load("sync", x_fm[:, c, tb:tb + 512])
                    k.wait("tensor", sem, ld)
                    T.matmul(PS[0][:, :], wq_t_[:, c, :], xt[slot], start=(c == 0), stop=(c == NCH - 1))
                    T.matmul(PS[1][:, :], wk_t_[:, c, :], xt[slot], start=(c == 0), stop=(c == NCH - 1))
                    for s_ in range(4):
                        mi = T.matmul(PS[2 + s_][:, 0:128], xt[slot][:, s_ * 128:s_ * 128 + 128],
                                      wv_t_[:, c, :], start=(c == 0), stop=(c == NCH - 1))
                    xring.consumed(slot, S_MM, S_MM.inc(mi))
                mm_qkv = S_MM.n
                k.wait("vector", "MM", mm_qkv)
                V.tensor_scalar_min(qp_blk, PS[0][:, :], 0.0)
                V.tensor_scalar_min(kp_blk, PS[1][:, :], 0.0)
                k.wait("scalar", "DV", S_DV.n)
                A.activation(out=qp_blk, in_=qp_blk, func=AF.Exp)
                A.activation(out=kp_blk, in_=kp_blk, func=AF.Exp)
                k.wait("vector", "AC", S_AC.n)
                V.scalar_tensor_tensor(qp_blk, PS[0][:, :], 0.0, qp_blk, op0=ALU.max, op1=ALU.add)
                V.scalar_tensor_tensor(kp_blk, PS[1][:, :], 0.0, kp_blk, op0=ALU.max, op1=ALU.add)
                for s_ in range(4):
                    V.tensor_copy(vext_blk[:, s_, 0:128], PS[2 + s_][:, 0:128])
                prev_evac_dv = S_DV.n

                for s_ in range(4):
                    k.wait("tensor", "DV", prev_evac_dv)
                    for m_ in range(4):
                        pm = PS[3 + (m_ % 2)][:, (m_ // 2) * 129:(m_ // 2) * 129 + 129]
                        T.matmul(pm, qp_blk[:, s_ * 128:s_ * 128 + 128], mem_t[:, m_, :],
                                 start=True, stop=True)
                    T.transpose(PS[5][:, 0:128], kp_blk[:, s_ * 128:s_ * 128 + 128], ident[:])
                    mi = T.matmul(PS[5][:, 128:256], kp_blk[:, s_ * 128:s_ * 128 + 128],
                                  qp_blk[:, s_ * 128:s_ * 128 + 128], start=True, stop=True)
                    mm_t1 = S_MM.inc(mi)
                    k.wait("vector", "MM", mm_t1)
                    V.tensor_copy(kptok, PS[5][:, 0:128])
                    V.tensor_tensor(sT_t, PS[5][:, 128:256], keepT[:], op=ALU.mult)
                    dv1 = S_DV.n
                    k.wait("tensor", "DV", dv1)
                    T.matmul(PS[6][:, 0:129], qp_blk[:, s_ * 128:s_ * 128 + 128], sbS[:, :],
                             start=True, stop=False)
                    T.matmul(PS[6][:, 0:129], sT_t, vext_blk[:, s_, :], start=False, stop=True)
                    n_ = blk * 4 + s_
                    mi = T.matmul(PS[7][:, 0:129], kptok, vext_blk[:, s_, :],
                                  start=True, stop=True)
                    mm_loc = S_MM.inc(mi)
                    k.wait("vector", "MM", mm_loc)
                    V.tensor_tensor(sbS[:, :], sbS[:, :], PS[7][:, 0:129], op=ALU.add)
                    V.tensor_scalar_max(stat_tmp[:, 0:1], PS[6][:, 128:129], EPS_L0)
                    for m_ in range(4):
                        pm = PS[3 + (m_ % 2)][:, (m_ // 2) * 129:(m_ // 2) * 129 + 129]
                        V.tensor_scalar_max(stat_tmp[:, 2 + m_:3 + m_], pm[:, 128:129], EPS_L0)
                    V.reciprocal(stat_tmp[:, 1:2], stat_tmp[:, 0:1])
                    for m_ in range(4):
                        V.reciprocal(stat_tmp[:, 6 + m_:7 + m_], stat_tmp[:, 2 + m_:3 + m_])
                    V.tensor_scalar(lout, PS[6][:, 0:128], stat_tmp[:, 1:2], g1m_col, op0=ALU.mult, op1=ALU.mult)
                    for m_ in range(4):
                        pm = PS[3 + (m_ % 2)][:, (m_ // 2) * 129:(m_ // 2) * 129 + 129]
                        if m_ == 0:
                            V.tensor_scalar(macc, pm[:, 0:128], stat_tmp[:, 6:7], cg_col(0), op0=ALU.mult, op1=ALU.mult)
                        else:
                            V.tensor_scalar(mtmp, pm[:, 0:128], stat_tmp[:, 6 + m_:7 + m_], cg_col(m_), op0=ALU.mult, op1=ALU.mult)
                            V.tensor_tensor(macc, macc, mtmp, op=ALU.add)
                    V.tensor_tensor(lout, lout, macc, op=ALU.add)
                    dv6 = S_DV.n
                    k.wait("tensor", "DV", dv6)
                    mi = T.transpose(PS[5][:, 256:384], lout, ident[:])
                    mm_ot = S_MM.inc(mi)
                    k.wait("vector", "MM", mm_ot)
                    V.tensor_copy(ctx_all[:, h0, tb + s_ * 128:tb + s_ * 128 + 128], PS[5][:, 256:384])
                    prev_evac_dv = S_DV.n
            k.wait("sync", "MM", S_MM.n)
            k.wait("sync", "DV", S_DV.n)

        if DEBUG_TAPS:
            k.wait("gpsimd", "DV", S_DV.n)
            st_tap.dma("gpsimd", taps["tap_ctx0"][:], ctx_all[:, :, :], serial=True)

        arw2 = AR_RES[:].rearrange("p c t -> p (c t)")
        wo_sl = [arw2[:, 0:1024].rearrange("p (h i) -> p h i", h=8),
                 arw2[:, 1024:2048].rearrange("p (h i) -> p h i", h=8)]
        obuf = [arw2[:, 2048 + i * 512: 2048 + (i + 1) * 512] for i in range(4)]
        k.wait("sync", "MM", S_MM.n)
        k.wait("sync", "DV", S_DV.n)
        wo_free = {0: 0, 1: 0}
        ob_free = {i: 0 for i in range(4)}
        ps_free = {0: 0, 1: 0}
        for oc in range(16):
            slot = oc % 2
            k.wait("sync", "MM", wo_free[slot])
            ldtok = st_wo[slot].dma("sync", wo_sl[slot], a_wo[:, :, oc, :])
            k.wait("tensor", st_wo[slot].sem, ldtok)
            for tb4 in range(4):
                bslot = (oc * 4 + tb4) % 2
                k.wait("tensor", "DV", ps_free[bslot])
                for h0 in range(8):
                    mi = T.matmul(PS[bslot][:, :], wo_sl[slot][:, h0, :], ctx_all[:, h0, tb4 * 512:(tb4 + 1) * 512],
                                  start=(h0 == 0), stop=(h0 == 7))
                mm = S_MM.inc(mi)
                oslot = (oc * 4 + tb4) % 4
                k.wait("vector", "MM", mm)
                k.wait("vector", st_ob[oslot].sem, ob_free[oslot])
                V.tensor_copy(obuf[oslot], PS[bslot][:, :])
                ps_free[bslot] = S_DV.n
                k.wait("gpsimd", "DV", S_DV.n)
                t0 = tb4 * 512
                ob_free[oslot] = st_ob[oslot].dma("gpsimd", cc0_in[t0 // TOK, oc, :, t0 % TOK:t0 % TOK + 512], obuf[oslot])
            wo_free[slot] = S_MM.n
        for i in range(4):
            k.wait("gpsimd", st_ob[i].sem, st_ob[i].sem.n)
        cc = S_CC.inc(nc.gpsimd.collective_compute(
            "ReduceScatter", ALU.add,
            replica_groups=[[0, 1], [2, 3], [4, 5], [6, 7]],
            ins=[cc0_in[:]], outs=[cc0_out[:]]))

        k.wait("sync", "CC", cc)
        k.wait("vector", "MM", S_MM.n)
        for i in range(4):
            k.wait("vector", st_ob[i].sem, st_ob[i].sem.n)
        arst = AR_RES[:]
        for tb2 in range(2):
            for c in range(NCH):
                s0, sem0, ld0 = xring.load("sync", x_own[:, c, tb2 * 512:(tb2 + 1) * 512])
                s1, sem1, ld1 = xring.load("sync", cc0_out[c, :, tb2 * 512:(tb2 + 1) * 512])
                k.wait("vector", sem0, ld0)
                k.wait("vector", sem1, ld1)
                V.tensor_tensor(arst[:, c, tb2 * 512:(tb2 + 1) * 512], xt[s0], xt[s1], op=ALU.add)
                xring.consumed(s0, S_DV, S_DV.n)
                if DEBUG_TAPS:
                    k.wait("gpsimd", "DV", S_DV.n)
                    ttok = st_tap.dma("gpsimd", taps["tap_a0"][c, :, tb2 * 512:(tb2 + 1) * 512], xt[s1], serial=True)
                    xring.consumed(s1, st_tap.sem, ttok)
                else:
                    xring.consumed(s1, S_DV, S_DV.n)
        for tb2 in range(2):
            ln_fm(src_tiles=lambda c, tb2=tb2: arst[:, c, tb2 * 512:(tb2 + 1) * 512],
                  dst_tiles=lambda c, tb2=tb2: arst[:, c, tb2 * 512:(tb2 + 1) * 512],
                  psA=PS[0], psB=PS[1], nt=512)
        if DEBUG_TAPS:
            k.wait("gpsimd", "DV", S_DV.n)
            for c in range(NCH):
                st_tap.dma("gpsimd", taps["tap_x1"][c], arst[:, c, :], serial=True)

        y_t = AR_BIG[:].rearrange("p (o t) -> p o t", o=16)
        hh = AR_W[:, 0:5632].rearrange("p (i t) -> p i t", i=11)
        w1fl = [RING[:, 0:1024], RING[:, 1024:2048]]
        w3fl = [RING[:, 2048:3072], RING[:, 3072:4096]]
        w2fl = [RING[:, 4096:4864], RING[:, 4864:5632]]
        w1v = [t.rearrange("p (c i) -> p c i", c=8) for t in w1fl]
        w3v = [t.rearrange("p (c i) -> p c i", c=8) for t in w3fl]
        w2v = [t.rearrange("p (c i) -> p c i", c=6) for t in w2fl]
        w1r = Ring(k, "w1r", w1fl)
        w3r = Ring(k, "w3r", w3fl)
        w2r = Ring(k, "w2r", w2fl)
        werep = AR_W[:, 5632:6144]
        werow0 = rowbuf[0:1, 0:512]

        units = [("sh", 0), ("sh", 1)]
        if not SPARSE_MOE:
            units = units + [("exp", e) for e in range(16)]

        k.wait("sync", "DV", S_DV.n)
        k.wait("sync", "MM", S_MM.n)
        if DEBUG_TAPS:
            k.wait("sync", st_tap.sem, st_tap.sem.n)
        ps01_free = {0: 0, 1: 0}
        ps_y_free = {2: 0, 3: 0}
        for th in range(2):
            t0 = th * 512
            x1h = lambda c: arst[:, c, t0:t0 + 512]
            for st_ in range(4):
                k.wait("tensor", "DV", S_DV.n)
                k.wait("tensor", "AC", S_AC.n)
                for c in range(NCH):
                    mi = T.matmul(PS[4][:, 0:16], arst[:, c, t0 + st_ * 128:t0 + st_ * 128 + 128],
                                  gwT_sb[:, c, :], start=(c == 0), stop=(c == NCH - 1))
                mm = S_MM.inc(mi)
                k.wait("vector", "MM", mm)
                V.reduce_max(router[:, 64:65], PS[4][:, 0:16], axis=AX.X, negate=True)
                k.wait("scalar", "DV", S_DV.n)
                A.activation(out=router[:, 0:16], in_=PS[4][:, 0:16], func=AF.Exp,
                             bias=router[:, 64:65], scale=1.0, accum_out=router[:, 65:66])
                k.wait("vector", "AC", S_AC.n)
                V.reciprocal(router[:, 66:67], router[:, 65:66])
                V.tensor_scalar_mul(router[:, 0:16], router[:, 0:16], router[:, 66:67])
                V.reduce_max(router[:, 67:68], router[:, 0:16], axis=AX.X)
                V.tensor_scalar(router[:, 16:32], router[:, 0:16], router[:, 67:68], None, op0=ALU.is_ge)
                V.tensor_tensor_scan(router[:, 32:48], router[:, 16:32], router[:, 16:32], zcol[:],
                                     op0=ALU.add, op1=ALU.bypass)
                V.tensor_scalar(router[:, 48:64], router[:, 32:48], 1.0, None, op0=ALU.is_equal)
                V.tensor_tensor(router[:, 16:32], router[:, 16:32], router[:, 48:64], op=ALU.mult)
                V.scalar_tensor_tensor(router[:, 32:48], router[:, 16:32], -2.0, router[:, 0:16],
                                       op0=ALU.mult, op1=ALU.add)
                V.reduce_max(router[:, 68:69], router[:, 32:48], axis=AX.X)
                V.tensor_scalar(router[:, 48:64], router[:, 32:48], router[:, 68:69], None, op0=ALU.is_ge)
                V.tensor_tensor_scan(router[:, 32:48], router[:, 48:64], router[:, 48:64], zcol[:],
                                     op0=ALU.add, op1=ALU.bypass)
                V.tensor_scalar(router[:, 32:48], router[:, 32:48], 1.0, None, op0=ALU.is_equal)
                V.tensor_tensor(router[:, 48:64], router[:, 48:64], router[:, 32:48], op=ALU.mult)
                stg = th * 4 + st_
                if SPARSE_MOE:
                    V.tensor_copy(m1sb[:, stg * 16:(stg + 1) * 16], router[:, 16:32])
                    V.tensor_copy(m2sb[:, stg * 16:(stg + 1) * 16], router[:, 48:64])
                    V.tensor_copy(wv_sb[:, stg * 2:stg * 2 + 1], router[:, 67:68])
                    V.tensor_copy(wv_sb[:, stg * 2 + 1:stg * 2 + 2], router[:, 68:69])
                else:
                    V.tensor_scalar(router[:, 16:32], router[:, 16:32], router[:, 67:68], None, op0=ALU.mult)
                    V.tensor_scalar(router[:, 48:64], router[:, 48:64], router[:, 68:69], None, op0=ALU.mult)
                    V.tensor_tensor(westore[:, st_ * 16:(st_ + 1) * 16], router[:, 16:32], router[:, 48:64], op=ALU.add)
            dv_we = S_DV.n
            ps01_free = {0: dv_we, 1: dv_we}

            for ui, (ukind, ue) in enumerate(units):
                first_unit = (ui == 0)
                if ukind == "exp":
                    k.wait("tensor", "DV", S_DV.n)
                    k.wait("tensor", "AC", S_AC.n)
                    for st_ in range(4):
                        mi = T.transpose(PS[5][0:1, st_ * 128:(st_ + 1) * 128],
                                         westore[:, st_ * 16 + ue:st_ * 16 + ue + 1], ident[:])
                    mm = S_MM.inc(mi)
                    k.wait("vector", "MM", mm)
                    V.tensor_copy(werow0, PS[5][0:1, 0:512])
                    k.wait("tensor", "DV", S_DV.n)
                    mi = T.matmul(PS[4][:, 0:512], ones1x128[:], werow0, start=True, stop=True)
                    mm = S_MM.inc(mi)
                    k.wait("vector", "MM", mm)
                    V.tensor_copy(werep, PS[4][:, 0:512])
                    ps01_free = {0: S_DV.n, 1: S_DV.n}

                for it in range(11):
                    if ukind == "exp":
                        srcs1 = (f_w1[ue, :, it, 0:1024], f_w1[ue, :, it, 1024:2048])
                        srcs3 = (f_w3[ue, :, it, 0:1024], f_w3[ue, :, it, 1024:2048])
                    else:
                        it22 = ue * 11 + it
                        srcs1 = (f_sw1[:, it22, 0:1024], f_sw1[:, it22, 1024:2048])
                        srcs3 = (f_sw3[:, it22, 0:1024], f_sw3[:, it22, 1024:2048])
                    sA, semA, ldA = w1r.load("sync", srcs1[0])
                    sB, semB, ldB = w1r.load("sync", srcs1[1])
                    s3A, sem3A, ld3A = w3r.load("sync", srcs3[0])
                    s3B, sem3B, ld3B = w3r.load("sync", srcs3[1])
                    bp1, bp3 = (0, 1) if it % 2 == 0 else (4, 5)
                    k.wait("tensor", "DV", ps01_free[it % 2])
                    k.wait("tensor", semA, ldA)
                    for c8 in range(8):
                        T.matmul(PS[bp1][:, :], w1v[sA][:, c8, :], x1h(c8), start=(c8 == 0), stop=False)
                    k.wait("tensor", semB, ldB)
                    for c8 in range(8):
                        mi = T.matmul(PS[bp1][:, :], w1v[sB][:, c8, :], x1h(8 + c8), start=False, stop=(c8 == 7))
                    mmw1 = S_MM.inc(mi)
                    w1r.consumed(sA, S_MM, mmw1)
                    w1r.consumed(sB, S_MM, mmw1)
                    k.wait("tensor", sem3A, ld3A)
                    for c8 in range(8):
                        T.matmul(PS[bp3][:, :], w3v[s3A][:, c8, :], x1h(c8), start=(c8 == 0), stop=False)
                    k.wait("tensor", sem3B, ld3B)
                    for c8 in range(8):
                        mi = T.matmul(PS[bp3][:, :], w3v[s3B][:, c8, :], x1h(8 + c8), start=False, stop=(c8 == 7))
                    mmw3 = S_MM.inc(mi)
                    w3r.consumed(s3A, S_MM, mmw3)
                    w3r.consumed(s3B, S_MM, mmw3)
                    k.wait("scalar", "MM", mmw3)
                    A.activation(out=stat_tmp[:, (it % 2) * 512:(it % 2) * 512 + 512], in_=PS[bp1][:, :], func=AF.Silu)
                    k.wait("vector", "AC", S_AC.n)
                    V.tensor_tensor(hh[:, it, :], stat_tmp[:, (it % 2) * 512:(it % 2) * 512 + 512], PS[bp3][:, :], op=ALU.mult)
                    if ukind == "exp":
                        V.tensor_tensor(hh[:, it, :], hh[:, it, :], werep, op=ALU.mult)
                    ps01_free[it % 2] = S_DV.n

                dv_hh = S_DV.n
                for ot in range(16):
                    if ukind == "exp":
                        src2 = (f_w2[ue, :, ot, 0:768], f_w2[ue, :, ot, 768:1408])
                    else:
                        src2 = (f_sw2[:, ot, ue * 1408:ue * 1408 + 768],
                                f_sw2[:, ot, ue * 1408 + 768:ue * 1408 + 1408])
                    s2A, sem2A, ld2A = w2r.load("sync", src2[0])
                    s2B, sem2B, ld2B = w2r.load("sync", src2[1], out_view=lambda s: w2v[s][:, 0:5, :])
                    bank = 2 + (ot % 2)
                    k.wait("tensor", "DV", max(ps_y_free[bank], dv_hh))
                    k.wait("tensor", sem2A, ld2A)
                    for i6 in range(6):
                        T.matmul(PS[bank][:, :], w2v[s2A][:, i6, :], hh[:, i6, :], start=(i6 == 0), stop=False)
                    k.wait("tensor", sem2B, ld2B)
                    for i5 in range(5):
                        mi = T.matmul(PS[bank][:, :], w2v[s2B][:, i5, :], hh[:, 6 + i5, :],
                                      start=False, stop=(i5 == 4))
                    mmw2 = S_MM.inc(mi)
                    w2r.consumed(s2A, S_MM, mmw2)
                    w2r.consumed(s2B, S_MM, mmw2)
                    k.wait("vector", "MM", mmw2)
                    if first_unit:
                        V.tensor_copy(y_t[:, ot, t0:t0 + 512], PS[bank][:, :])
                    else:
                        V.tensor_tensor(y_t[:, ot, t0:t0 + 512], y_t[:, ot, t0:t0 + 512], PS[bank][:, :], op=ALU.add)
                    ps_y_free[bank] = S_DV.n

        if SPARSE_MOE:
            st_idx = k.stream("stidx")
            g_ = nc.gpsimd
            k.wait("tensor", st_c.sem, const_tok)
            k.wait("tensor", "DV", S_DV.n)
            k.wait("tensor", "AC", S_AC.n)
            mi = T.matmul(PS[5][:, 0:16], ones1x128[:], eoffrow[:], start=True, stop=True)
            mm = S_MM.inc(mi)
            k.wait("vector", "MM", mm)
            V.tensor_copy(eoffrep[:], PS[5][:, 0:16])
            V.memset(pinit[:], 1024)
            V.memset(idx16[:], 0)
            V.memset(baserow[:], 0.0)
            V.memset(stat_tmp[:, 0:512], 0.0)
            dv0 = S_DV.n
            k.wait("gpsimd", "DV", dv0)
            st_z = k.stream("stz")
            _zr = (lambda *a, **kw: 0) if SKIP_EXPERTS >= 4 else st_z.dma
            k.wait("sync", "DV", dv0)
            _zr("sync", perm_hbm[:, 0].rearrange("(a b) -> a b", a=128), pinit[:])
            _zr("sync", wslot_hbm[:, 0].rearrange("(a b) -> a b", a=128), stat_tmp[:, 0:32])
            for cz in range(4):
                _zr("sync", x1tm[1024:1025, cz * 512:(cz + 1) * 512], stat_tmp[0:1, 0:512])
            for rz in range(9):
                rows = 128 if rz < 8 else 1
                for cz in range(4):
                    _zr("sync", ymoe[rz * 128:rz * 128 + rows, cz * 512:(cz + 1) * 512], stat_tmp[0:rows, 0:512])
            ztok = st_z.sem.n

            st_x1 = [k.stream("stx1a"), k.stream("stx1b")]
            x1s_free = {0: 0, 1: 0}
            k.wait("tensor", "DV", S_DV.n)
            for tt in range(0 if SKIP_EXPERTS >= 3 else 8):
                for c in range(NCH):
                    sl2 = (tt * NCH + c) % 2
                    k.wait("tensor", st_x1[sl2].sem, x1s_free[sl2])
                    mi = T.transpose(PS[5][:, sl2 * 128:(sl2 + 1) * 128], arst[:, c, tt * 128:(tt + 1) * 128], ident[:])
                    mm = S_MM.inc(mi)
                    k.wait("vector", "MM", mm)
                    k.wait("vector", st_x1[sl2].sem, x1s_free[sl2])
                    V.tensor_copy(stat_tmp[:, 512 + sl2 * 128:512 + (sl2 + 1) * 128], PS[5][:, sl2 * 128:(sl2 + 1) * 128])
                    k.wait("sync", "DV", S_DV.n)
                    x1s_free[sl2] = st_x1[sl2].dma("sync", x1tm[tt * 128:(tt + 1) * 128, c * 128:(c + 1) * 128],
                                                   stat_tmp[:, 512 + sl2 * 128:512 + (sl2 + 1) * 128])

            sw_guard = 0
            for stg in range(0 if SKIP_EXPERTS >= 2 else 8):
                k.wait("tensor", "DV", S_DV.n)
                msk1 = m1sb[:, stg * 16:(stg + 1) * 16]
                msk2 = m2sb[:, stg * 16:(stg + 1) * 16]
                V.tensor_tensor(sltmp[:, 0:16], msk1, msk2, op=ALU.add)
                dvm = S_DV.n
                k.wait("tensor", "DV", dvm)
                T.matmul(PS[4][:, 0:16], lstrict[:], sltmp[:, 0:16], start=True, stop=True)
                T.matmul(PS[4][0:1, 16:32], ones128[:], sltmp[:, 0:16], start=True, stop=True)
                mi = T.matmul(PS[5][:, 0:16], ones1x128[:], baserow[:], start=True, stop=True)
                mm = S_MM.inc(mi)
                k.wait("vector", "MM", mm)
                V.tensor_copy(sltmp[:, 16:32], PS[4][:, 0:16])
                V.tensor_tensor(sltmp[:, 16:32], sltmp[:, 16:32], PS[5][:, 0:16], op=ALU.add)
                V.tensor_tensor(sltmp[:, 16:32], sltmp[:, 16:32], eoffrep[:], op=ALU.add)
                V.tensor_tensor(baserow[:], baserow[:], PS[4][0:1, 16:32], op=ALU.add)
                V.tensor_tensor(sltmp[:, 32:48], msk1, sltmp[:, 16:32], op=ALU.mult)
                V.reduce_sum(sltmp[:, 48:49], sltmp[:, 32:48], axis=AX.X)
                V.tensor_tensor(sltmp[:, 32:48], msk2, sltmp[:, 16:32], op=ALU.mult)
                V.reduce_sum(sltmp[:, 49:50], sltmp[:, 32:48], axis=AX.X)
                k.wait("vector", S_SW, sw_guard)
                V.tensor_copy(slot32[:, 0:1], sltmp[:, 48:49])
                V.tensor_copy(slot32[:, 1:2], sltmp[:, 49:50])
                dvs = S_DV.n
                k.wait("gpsimd", "DV", dvs)
                k.wait("gpsimd", st_z.sem, ztok)
                from bass_rust import IndirectOffsetOnAxis as _IO
                S_SW.inc(g_.indirect_dma_start(out=perm_hbm[:], out_offset=_IO(ap=slot32[:, 0:1], axis=0),
                                               in_=tok16_sb[:, stg:stg + 1], in_offset=None), 16)
                k.wait("gpsimd", S_SW, S_SW.n)
                S_SW.inc(g_.indirect_dma_start(out=perm_hbm[:], out_offset=_IO(ap=slot32[:, 1:2], axis=0),
                                               in_=tok16_sb[:, stg:stg + 1], in_offset=None), 16)
                k.wait("gpsimd", S_SW, S_SW.n)
                S_SW.inc(g_.indirect_dma_start(out=wslot_hbm[:], out_offset=_IO(ap=slot32[:, 0:1], axis=0),
                                               in_=wv_sb[:, stg * 2:stg * 2 + 1], in_offset=None), 16)
                k.wait("gpsimd", S_SW, S_SW.n)
                S_SW.inc(g_.indirect_dma_start(out=wslot_hbm[:], out_offset=_IO(ap=slot32[:, 1:2], axis=0),
                                               in_=wv_sb[:, stg * 2 + 1:stg * 2 + 2], in_offset=None), 16)
                k.wait("gpsimd", S_SW, S_SW.n)
                sw_guard = S_SW.n
            sw_tables = S_SW.n

            xe_tok = AR_W[:, 0:4096].rearrange("p (g t) -> p g t", g=2)
            xe_fm = AR_W[:, 4096:8192].rearrange("p (c t) -> p c t", c=16)
            hh_s = AR_W[:, 0:2816].rearrange("p (i t) -> p i t", i=11)
            ysm = [stat_tmp[:, 0:256], stat_tmp[:, 256:512]]
            ysm_pre = stat_tmp[:, 768:1024]
            werep_s = stat_tmp[:, 512:768]
            ps01f = 0
            ps_yf = {2: 0, 3: 0}
            ysm_free = {0: 0, 1: 0}
            x1_tok = (st_x1[0].sem.n, st_x1[1].sem.n)
            k.wait("vector", st_x1[0].sem, x1_tok[0])
            k.wait("vector", st_x1[1].sem, x1_tok[1])
            reg256 = nc.gpsimd.alloc_register()
            nc.gpsimd.reg_mov(reg256, 256)
            for e in range(0 if SKIP_EXPERTS >= 1 else 16):
                k.wait("gpsimd", st_x1[0].sem, x1_tok[0])
                k.wait("gpsimd", st_x1[1].sem, x1_tok[1])
                k.wait("gpsimd", st_z.sem, ztok)
                k.wait("sync", S_SW, S_SW.n)
                l1 = st_idx.dma("sync", idx16[0:16, :], perm_hbm[e * 256:(e + 1) * 256, 0].rearrange("(s p) -> p s", p=16),
                                allow_slow_non_contiguous=True)
                l2 = st_idx.dma("sync", idx16[16:32, :], perm_hbm[e * 256:(e + 1) * 256, 0].rearrange("(s p) -> p s", p=16),
                                allow_slow_non_contiguous=True)
                lw = st_dw.dma("sync", werow0[0:1, 0:256],
                               wslot_hbm[e * 256:(e + 1) * 256, 0:1].rearrange("a b -> b a"), serial=True)
                k.wait("gpsimd", st_idx.sem, l2)
                k.wait("gpsimd", "MM", S_MM.n)
                k.wait("gpsimd", "DV", S_DV.n)
                S_SW.inc(g_.dma_gather(xe_tok[:, :, :], x1tm[:], idx16[:], num_idxs=256, num_idxs_reg=reg256,
                                       elem_size=2048), 16)
                k.wait("gpsimd", S_SW, S_SW.n)
                gtok = S_SW.n
                k.wait("tensor", st_dw.sem, lw)
                k.wait("tensor", "DV", S_DV.n)
                k.wait("tensor", "AC", S_AC.n)
                mi = T.matmul(PS[4][:, 0:256], ones1x128[:], werow0[0:1, 0:256], start=True, stop=True)
                mm = S_MM.inc(mi)
                k.wait("vector", "MM", mm)
                V.tensor_copy(werep_s, PS[4][:, 0:256])
                k.wait("tensor", S_SW, gtok)
                for g2 in range(2):
                    for c in range(NCH):
                        mi = T.transpose(PS[5][:, (c % 4) * 128:(c % 4 + 1) * 128], xe_tok[:, g2, c * 128:(c + 1) * 128], ident[:])
                        if c % 4 == 3:
                            mm = S_MM.inc(mi)
                            k.wait("vector", "MM", mm)
                            for cc in range(c - 3, c + 1):
                                V.tensor_copy(xe_fm[:, cc, g2 * 128:(g2 + 1) * 128], PS[5][:, (cc % 4) * 128:(cc % 4 + 1) * 128])
                            k.wait("tensor", "DV", S_DV.n)
                dv_xe = S_DV.n
                for it in range(11):
                    sA, semA, ldA = w1r.load("sync", f_w1[e, :, it, 0:1024])
                    sB, semB, ldB = w1r.load("sync", f_w1[e, :, it, 1024:2048])
                    s3A, sem3A, ld3A = w3r.load("sync", f_w3[e, :, it, 0:1024])
                    s3B, sem3B, ld3B = w3r.load("sync", f_w3[e, :, it, 1024:2048])
                    k.wait("tensor", "DV", max(ps01f, dv_xe))
                    k.wait("tensor", semA, ldA)
                    for c8 in range(8):
                        T.matmul(PS[0][:, 0:256], w1v[sA][:, c8, :], xe_fm[:, c8, :], start=(c8 == 0), stop=False)
                    k.wait("tensor", semB, ldB)
                    for c8 in range(8):
                        mi = T.matmul(PS[0][:, 0:256], w1v[sB][:, c8, :], xe_fm[:, 8 + c8, :], start=False, stop=(c8 == 7))
                    mmw1 = S_MM.inc(mi)
                    w1r.consumed(sA, S_MM, mmw1)
                    w1r.consumed(sB, S_MM, mmw1)
                    k.wait("tensor", sem3A, ld3A)
                    for c8 in range(8):
                        T.matmul(PS[1][:, 0:256], w3v[s3A][:, c8, :], xe_fm[:, c8, :], start=(c8 == 0), stop=False)
                    k.wait("tensor", sem3B, ld3B)
                    for c8 in range(8):
                        mi = T.matmul(PS[1][:, 0:256], w3v[s3B][:, c8, :], xe_fm[:, 8 + c8, :], start=False, stop=(c8 == 7))
                    mmw3 = S_MM.inc(mi)
                    w3r.consumed(s3A, S_MM, mmw3)
                    w3r.consumed(s3B, S_MM, mmw3)
                    k.wait("scalar", "MM", mmw3)
                    k.wait("scalar", "DV", ps01f)
                    A.activation(out=stat_tmp[:, 768:1024], in_=PS[0][:, 0:256], func=AF.Silu)
                    k.wait("vector", "AC", S_AC.n)
                    V.tensor_tensor(hh_s[:, it, :], stat_tmp[:, 768:1024], PS[1][:, 0:256], op=ALU.mult)
                    V.tensor_tensor(hh_s[:, it, :], hh_s[:, it, :], werep_s, op=ALU.mult)
                    ps01f = S_DV.n
                dv_hh2 = S_DV.n
                for ot in range(16):
                    s2A, sem2A, ld2A = w2r.load("sync", f_w2[e, :, ot, 0:768])
                    s2B, sem2B, ld2B = w2r.load("sync", f_w2[e, :, ot, 768:1408], out_view=lambda s: w2v[s][:, 0:5, :])
                    bank = 2 + (ot % 2)
                    k.wait("tensor", "DV", max(ps_yf[bank], dv_hh2))
                    k.wait("tensor", sem2A, ld2A)
                    for i6 in range(6):
                        T.matmul(PS[bank][:, 0:256], w2v[s2A][:, i6, :], hh_s[:, i6, :], start=(i6 == 0), stop=False)
                    k.wait("tensor", sem2B, ld2B)
                    for i5 in range(5):
                        mi = T.matmul(PS[bank][:, 0:256], w2v[s2B][:, i5, :], hh_s[:, 6 + i5, :], start=False, stop=(i5 == 4))
                    mmw2 = S_MM.inc(mi)
                    w2r.consumed(s2A, S_MM, mmw2)
                    w2r.consumed(s2B, S_MM, mmw2)
                    ybuf = ot % 2
                    k.wait("vector", "MM", mmw2)
                    k.wait("vector", S_SW, ysm_free[ybuf])
                    V.tensor_copy(ysm_pre, PS[bank][:, 0:256])
                    ps_yf[bank] = S_DV.n
                    dvp = S_DV.n
                    k.wait("tensor", "DV", dvp)
                    for g2 in range(2):
                        mi = T.transpose(PS[5][:, g2 * 128:(g2 + 1) * 128], ysm_pre[:, g2 * 128:(g2 + 1) * 128], ident[:])
                    mm = S_MM.inc(mi)
                    k.wait("vector", "MM", mm)
                    V.tensor_copy(ysm[ybuf][:, 0:128], PS[5][:, 0:128])
                    V.tensor_copy(ysm[ybuf][:, 128:256], PS[5][:, 128:256])
                    dvy = S_DV.n
                    k.wait("gpsimd", "DV", dvy)
                    S_SW.inc(g_.dma_scatter_add(ymoe[:, ot * 128:(ot + 1) * 128],
                                                ysm[ybuf].rearrange("p (g t) -> p g t", g=2), idx16[:],
                                                num_idxs=256, num_idxs_reg=reg256, elem_size=128, elem_step=2048), 16)
                    k.wait("gpsimd", S_SW, S_SW.n)
                    ysm_free[ybuf] = S_SW.n
            k.wait("sync", S_SW, S_SW.n)

        k.wait("vector", "MM", S_MM.n)
        if SPARSE_MOE and SKIP_EXPERTS < 4:
            k.wait("sync", st_z.sem, st_z.sem.n)
            k.wait("sync", st_x1[0].sem, st_x1[0].sem.n)
            k.wait("sync", st_x1[1].sem, st_x1[1].sem.n)
            ps_cb = 0
            for tt in range(8):
                for c in range(NCH):
                    slot, sem, ld = xring.load("sync", ymoe[tt * 128:(tt + 1) * 128, c * 128:(c + 1) * 128],
                                               out_view=lambda s: xt[s][:, 0:128])
                    k.wait("tensor", sem, ld)
                    k.wait("tensor", "DV", ps_cb)
                    mi = T.transpose(PS[5][:, 0:128], xt[slot][:, 0:128], ident[:])
                    mm = S_MM.inc(mi)
                    xring.consumed(slot, S_MM, mm)
                    k.wait("vector", "MM", mm)
                    sl3 = arst[:, c, tt * 128:(tt + 1) * 128]
                    V.tensor_tensor(stat_tmp[:, 0:128], sl3, y_t[:, c, tt * 128:(tt + 1) * 128], op=ALU.add)
                    V.tensor_tensor(sl3, stat_tmp[:, 0:128], PS[5][:, 0:128], op=ALU.add)
                    ps_cb = S_DV.n
        else:
            for th in range(2):
                t0 = th * 512
                for c in range(NCH):
                    V.tensor_tensor(arst[:, c, t0:t0 + 512], arst[:, c, t0:t0 + 512], y_t[:, c, t0:t0 + 512], op=ALU.add)
            k.wait("sync", "DV", S_DV.n)
        for th in range(2):
            ln_fm(src_tiles=lambda c, th=th: arst[:, c, th * 512:(th + 1) * 512],
                  dst_tiles=lambda c, th=th: arst[:, c, th * 512:(th + 1) * 512],
                  psA=PS[0], psB=PS[1], nt=512)
        if DEBUG_TAPS:
            k.wait("gpsimd", "DV", S_DV.n)
            for c in range(NCH):
                st_tap.dma("gpsimd", taps["tap_x2"][c], arst[:, c, :], serial=True)

        k.wait("gpsimd", "DV", S_DV.n)
        for c in range(NCH):
            st_cc1.dma("gpsimd", cc1_in[c], arst[:, c, :])
        k.wait("gpsimd", st_cc1.sem, st_cc1.sem.n)
        cc1tok = S_CC.inc(nc.gpsimd.collective_compute(
            "AllGather", ALU.bypass,
            replica_groups=[[0, 1], [2, 3], [4, 5], [6, 7]],
            ins=[cc1_in[:]], outs=[cc1_out[:]]))

        st_cq = k.stream("stcq")
        ckv = AR_W[:, 0:8192].rearrange("p (kc t) -> p kc t", kc=4)
        k.wait("sync", "CC", S_CC.n)
        k.wait("sync", "MM", S_MM.n)
        k.wait("sync", "DV", S_DV.n)
        k.wait("sync", st_tap.sem, st_tap.sem.n)
        arw3_d0 = AR_RES[:].rearrange("p c t -> p (c t)")
        wti = arw3_d0[:, 0:2048].rearrange("p (c i) -> p c i", c=16)
        wti2 = arw3_d0[:, 2048:4096].rearrange("p (c i) -> p c i", c=16)
        ps_ev = 0
        wsl_free = {0: 0, 1: 0}
        for qt in range(16):
            wsl = [wti, wti2][qt % 2]
            k.wait("sync", "MM", wsl_free[qt % 2])
            if qt < 4:
                wtok = st_dw.dma("sync", wsl, d_wdkv[:, :, qt * 128:(qt + 1) * 128], serial=True)
                bias_col = bias_sb[:, qt:qt + 1]
            else:
                wtok = st_dw.dma("sync", wsl, d_wdq[:, :, (qt - 4) * 128:(qt - 3) * 128], serial=True)
                bias_col = bias_sb[:, qt:qt + 1]
            k.wait("tensor", st_dw.sem, wtok)
            for tb in range(4):
                for c in range(NCH):
                    half = (tb * 512) // TOK
                    tl = (tb * 512) % TOK
                    slot, sem, ld = xring.load("sync", cc1_out[half, c, :, tl:tl + 512])
                    k.wait("tensor", sem, ld)
                    if c == 0:
                        k.wait("tensor", "DV", ps_ev)
                    mi = T.matmul(PS[tb % 2][:, :], wsl[:, c, :], xt[slot], start=(c == 0), stop=(c == NCH - 1))
                    xring.consumed(slot, S_MM, S_MM.inc(mi))
                k.wait("vector", "MM", S_MM.n)
                if qt < 4:
                    V.tensor_scalar(ckv[:, qt, tb * 512:(tb + 1) * 512], PS[tb % 2][:, :], bias_col, None, op0=ALU.add)
                    ps_ev = S_DV.n
                else:
                    k.wait("vector", st_cq.sem, st_cq.sem.n)
                    V.tensor_scalar(stat_tmp[:, 0:512], PS[tb % 2][:, :], bias_col, None, op0=ALU.add)
                    ps_ev = S_DV.n
                    k.wait("gpsimd", "DV", ps_ev)
                    st_cq.dma("gpsimd", cq_hbm[:, qt - 4, tb * 512:(tb + 1) * 512], stat_tmp[:, 0:512], serial=True)
            wsl_free[qt % 2] = S_MM.n

        arw3 = AR_RES[:].rearrange("p c t -> p (c t)")
        kr_b = arw3[:, 0:2048]
        kR_b = arw3[:, 2048:4096]
        kC_b = arw3[:, 4096:6144]
        vtok = arw3[:, 6144:8192].rearrange("p (s i) -> p s i", s=16)
        qC_b = arw3[:, 8192:8704]
        qR_b = arw3[:, 8704:9216]
        attn_b = arw3[:, 9216:11264]
        attnT_b = arw3[:, 11264:11392]
        bvrep = arw3[:, 11392:11520]
        rotb = arw3[:, 11520:12032]
        whd = arw3[:, 12032:14080].rearrange("p (c i) -> p c i", c=16)
        k.wait("sync", st_cq.sem, st_cq.sem.n)
        ctx_all2 = ctx_all
        for h0 in range(HPC):
            k.wait("sync", "MM", S_MM.n)
            btok = st_dw.dma("sync", bvrow[:], d_buv_row[h0:h0 + 1, :], serial=True)
            k.wait("tensor", st_dw.sem, btok)
            k.wait("tensor", "DV", S_DV.n)
            k.wait("tensor", "AC", S_AC.n)
            mi = T.matmul(PS[5][:, 0:128], ones1x128[:], bvrow[:], start=True, stop=True)
            mm = S_MM.inc(mi)
            k.wait("vector", "MM", mm)
            V.tensor_copy(bvrep, PS[5][:, 0:128])

            k.wait("sync", "MM", S_MM.n)
            wtok = st_dw.dma("sync", whd, d_wkr[h0], serial=True)
            k.wait("tensor", st_dw.sem, wtok)
            for tb in range(4):
                k.wait("tensor", "DV", S_DV.n)
                for c in range(NCH):
                    half = (tb * 512) // TOK
                    tl = (tb * 512) % TOK
                    slot, sem, ld = xring.load("sync", cc1_out[half, c, :, tl:tl + 512])
                    k.wait("tensor", sem, ld)
                    mi = T.matmul(PS[0][:, :], whd[:, c, :], xt[slot], start=(c == 0), stop=(c == NCH - 1))
                    xring.consumed(slot, S_MM, S_MM.inc(mi))
                k.wait("vector", "MM", S_MM.n)
                V.tensor_scalar(kr_b[:, tb * 512:(tb + 1) * 512], PS[0][:, :],
                                bias_sb[:, 24 + h0:25 + h0], None, op0=ALU.add)
                dvv = S_DV.n
                k.wait("tensor", "DV", dvv)
                mi = T.matmul(PS[1][:, :], rotm[:], kr_b[:, tb * 512:(tb + 1) * 512], start=True, stop=True)
                mm = S_MM.inc(mi)
                k.wait("vector", "MM", mm)
                V.tensor_copy(rotb, PS[1][:, :])
                sA_, semA_, ldA_ = xring.load("sync", c_ropeA[:, tb * 512:(tb + 1) * 512])
                sB_, semB_, ldB_ = xring.load("sync", c_ropeB[:, tb * 512:(tb + 1) * 512])
                k.wait("vector", semA_, ldA_)
                k.wait("vector", semB_, ldB_)
                V.tensor_tensor(stat_tmp[:, 0:512], kr_b[:, tb * 512:(tb + 1) * 512], xt[sA_], op=ALU.mult)
                V.tensor_tensor(rotb, rotb, xt[sB_], op=ALU.mult)
                V.tensor_tensor(kR_b[:, tb * 512:(tb + 1) * 512], stat_tmp[:, 0:512], rotb, op=ALU.add)
                xring.consumed(sA_, S_DV, S_DV.n)
                xring.consumed(sB_, S_DV, S_DV.n)

            k.wait("sync", "MM", S_MM.n)
            wtok = st_dw.dma("sync", whd[:, 0:4, :], d_wuk[h0], serial=True)
            k.wait("tensor", st_dw.sem, wtok)
            for tb in range(4):
                k.wait("tensor", "DV", S_DV.n)
                for kc in range(4):
                    mi = T.matmul(PS[0][:, :], whd[:, kc, :], ckv[:, kc, tb * 512:(tb + 1) * 512],
                                  start=(kc == 0), stop=(kc == 3))
                mm = S_MM.inc(mi)
                k.wait("vector", "MM", mm)
                V.tensor_scalar(kC_b[:, tb * 512:(tb + 1) * 512], PS[0][:, :],
                                bias_sb[:, 16 + h0:17 + h0], None, op0=ALU.add)

            k.wait("sync", "MM", S_MM.n)
            wtok = st_dw.dma("sync", whd[:, 0:4, :], d_wuv[h0], serial=True)
            k.wait("tensor", st_dw.sem, wtok)
            for s_ in range(16):
                k.wait("tensor", "DV", S_DV.n)
                for kc in range(4):
                    mi = T.matmul(PS[0][:, 0:128], ckv[:, kc, s_ * 128:(s_ + 1) * 128], whd[:, kc, :],
                                  start=(kc == 0), stop=(kc == 3))
                mm = S_MM.inc(mi)
                k.wait("vector", "MM", mm)
                V.tensor_tensor(vtok[:, s_, :], PS[0][:, 0:128], bvrep, op=ALU.add)

            k.wait("sync", "MM", S_MM.n)
            wtok = st_dw.dma("sync", whd[:, 0:12, :], d_wuq[h0], serial=True)
            k.wait("tensor", st_dw.sem, wtok)
            ps_sc_free = {2: 0, 3: 0}
            ps_ctx_free = 0
            for gi in range(4):
                k.wait("tensor", "DV", S_DV.n)
                for qc in range(12):
                    slot, sem, ld = xring.load("sync", cq_hbm[:, qc, gi * 512:(gi + 1) * 512])
                    k.wait("tensor", sem, ld)
                    mi = T.matmul(PS[0][:, :], whd[:, qc, :], xt[slot], start=(qc == 0), stop=(qc == 11))
                    xring.consumed(slot, S_MM, S_MM.inc(mi))
                k.wait("vector", "MM", S_MM.n)
                V.tensor_scalar(qC_b, PS[0][:, :], bias_sb[:, 32 + h0:33 + h0], None, op0=ALU.add)
                dvq = S_DV.n
                k.wait("tensor", "DV", dvq)
                mi = T.matmul(PS[1][:, :], rotm[:], qC_b, start=True, stop=True)
                mm = S_MM.inc(mi)
                k.wait("vector", "MM", mm)
                V.tensor_copy(rotb, PS[1][:, :])
                sA_, semA_, ldA_ = xring.load("sync", c_ropeA[:, gi * 512:(gi + 1) * 512])
                sB_, semB_, ldB_ = xring.load("sync", c_ropeB[:, gi * 512:(gi + 1) * 512])
                k.wait("vector", semA_, ldA_)
                k.wait("vector", semB_, ldB_)
                V.tensor_tensor(stat_tmp[:, 0:512], qC_b, xt[sA_], op=ALU.mult)
                V.tensor_tensor(rotb, rotb, xt[sB_], op=ALU.mult)
                V.tensor_tensor(qR_b, stat_tmp[:, 0:512], rotb, op=ALU.add)
                xring.consumed(sA_, S_DV, S_DV.n)
                xring.consumed(sB_, S_DV, S_DV.n)
                dvq2 = S_DV.n

                for iq in range(4):
                    i = gi * 4 + iq
                    qoff = iq * 128
                    L = (i + 1) * 128
                    nblk_k = (i // 4) + 1
                    k.wait("tensor", "DV", dvq2)
                    for jb in range(nblk_k):
                        bank = 2 + (jb % 2)
                        k.wait("tensor", "DV", ps_sc_free[bank])
                        T.matmul(PS[bank][:, :], qC_b[:, qoff:qoff + 128], kC_b[:, jb * 512:(jb + 1) * 512],
                                 start=True, stop=False)
                        mi = T.matmul(PS[bank][:, :], qR_b[:, qoff:qoff + 128], kR_b[:, jb * 512:(jb + 1) * 512],
                                      start=False, stop=True)
                        mm = S_MM.inc(mi)
                        k.wait("vector", "MM", mm)
                        hi = min(512, L - jb * 512)
                        full_cols = min(512, (i - jb * 4) * 128)
                        if full_cols > 0:
                            V.tensor_scalar_mul(attn_b[:, jb * 512:jb * 512 + full_cols],
                                                PS[bank][:, 0:full_cols], 0.0625)
                        if full_cols < hi:
                            V.scalar_tensor_tensor(attn_b[:, jb * 512 + full_cols:jb * 512 + full_cols + 128],
                                                   PS[bank][:, full_cols:full_cols + 128], 0.0625, trimask[:],
                                                   op0=ALU.mult, op1=ALU.add)
                        ps_sc_free[bank] = S_DV.n
                    V.reduce_max(stat_tmp[:, 512:513], attn_b[:, 0:L], axis=AX.X, negate=True)
                    k.wait("scalar", "DV", S_DV.n)
                    A.activation(out=attn_b[:, 0:L], in_=attn_b[:, 0:L], func=AF.Exp,
                                 bias=stat_tmp[:, 512:513], scale=1.0, accum_out=stat_tmp[:, 513:514])
                    k.wait("vector", "AC", S_AC.n)
                    V.reciprocal(stat_tmp[:, 514:515], stat_tmp[:, 513:514])
                    V.tensor_scalar_mul(attn_b[:, 0:L], attn_b[:, 0:L], stat_tmp[:, 514:515])
                    dva = S_DV.n
                    k.wait("tensor", "DV", dva)
                    for j in range(i + 1):
                        mi = T.transpose(PS[5][:, 128:256], attn_b[:, j * 128:(j + 1) * 128], ident[:])
                        mm = S_MM.inc(mi)
                        k.wait("vector", "MM", mm)
                        k.wait("vector", "DV", S_DV.n)
                        V.tensor_copy(attnT_b, PS[5][:, 128:256])
                        dvt = S_DV.n
                        k.wait("tensor", "DV", dvt)
                        mi = T.matmul(PS[4][:, 0:128], vtok[:, j, :], attnT_b, start=(j == 0), stop=(j == i))
                    mm = S_MM.inc(mi)
                    k.wait("vector", "MM", mm)
                    V.tensor_copy(ctx_all2[:, h0, i * 128:(i + 1) * 128], PS[4][:, 0:128])
                    ps_ctx_free = S_DV.n

        k.wait("sync", "MM", S_MM.n)
        k.wait("sync", "DV", S_DV.n)
        wo_free = {0: 0, 1: 0}
        ob_free = {i: 0 for i in range(4)}
        ps_free = {0: 0, 1: 0}
        for oc in range(16):
            slot = oc % 2
            k.wait("sync", "MM", wo_free[slot])
            ldtok = st_wo[slot].dma("sync", wo_sl[slot], d_wo[:, :, oc, :])
            k.wait("tensor", st_wo[slot].sem, ldtok)
            for tb4 in range(4):
                bslot = (oc * 4 + tb4) % 2
                k.wait("tensor", "DV", ps_free[bslot])
                for h0 in range(8):
                    mi = T.matmul(PS[bslot][:, :], wo_sl[slot][:, h0, :], ctx_all2[:, h0, tb4 * 512:(tb4 + 1) * 512],
                                  start=(h0 == 0), stop=(h0 == 7))
                mm = S_MM.inc(mi)
                oslot = (oc * 4 + tb4) % 4
                k.wait("vector", "MM", mm)
                k.wait("vector", st_ob[oslot].sem, ob_free[oslot])
                V.tensor_copy(obuf[oslot], PS[bslot][:, :])
                ps_free[bslot] = S_DV.n
                k.wait("gpsimd", "DV", S_DV.n)
                t0 = tb4 * 512
                ob_free[oslot] = st_ob[oslot].dma("gpsimd", cc2_in[t0 // TOK, oc, :, t0 % TOK:t0 % TOK + 512], obuf[oslot])
            wo_free[slot] = S_MM.n
        for i in range(4):
            k.wait("gpsimd", st_ob[i].sem, st_ob[i].sem.n)
        cc2tok = S_CC.inc(nc.gpsimd.collective_compute(
            "ReduceScatter", ALU.add,
            replica_groups=[[0, 1], [2, 3], [4, 5], [6, 7]],
            ins=[cc2_in[:]], outs=[cc2_out[:]]))

        k.wait("sync", "CC", S_CC.n)
        k.wait("vector", "MM", S_MM.n)
        for i in range(4):
            k.wait("vector", st_ob[i].sem, st_ob[i].sem.n)
        k.wait("vector", st_tap.sem, st_tap.sem.n)
        for tb2 in range(2):
            for c in range(NCH):
                s0, sem0, ld0 = xring.load("sync", cc1_in[c, :, tb2 * 512:(tb2 + 1) * 512])
                s1, sem1, ld1 = xring.load("sync", cc2_out[c, :, tb2 * 512:(tb2 + 1) * 512])
                k.wait("vector", sem0, ld0)
                k.wait("vector", sem1, ld1)
                V.tensor_tensor(stat_tmp[:, 0:512], xt[s0], xt[s1], op=ALU.add)
                V.tensor_scalar(arst[:, c, tb2 * 512:(tb2 + 1) * 512], stat_tmp[:, 0:512],
                                bias_sb[:, 40 + c:41 + c], None, op0=ALU.add)
                xring.consumed(s0, S_DV, S_DV.n)
                xring.consumed(s1, S_DV, S_DV.n)
        for tb2 in range(2):
            ln_fm(src_tiles=lambda c, tb2=tb2: arst[:, c, tb2 * 512:(tb2 + 1) * 512],
                  dst_tiles=lambda c, tb2=tb2: arst[:, c, tb2 * 512:(tb2 + 1) * 512],
                  psA=PS[0], psB=PS[1], nt=512)
        if DEBUG_TAPS:
            k.wait("gpsimd", "DV", S_DV.n)
            for c in range(NCH):
                st_tap.dma("gpsimd", taps["tap_x3"][c], arst[:, c, :], serial=True)

        eA_sb = RING[:, 0:2048].rearrange("p (c i) -> p c i", c=16)
        eB_sb = RING[:, 2048:4096].rearrange("p (c i) -> p c i", c=16)
        k.wait("sync", "MM", S_MM.n)
        k.wait("sync", "DV", S_DV.n)
        ltok = st_dw.dma("sync", eA_sb, e_A[:], serial=True)
        ltok2 = st_dw.dma("sync", eB_sb, e_B[:], serial=True)
        k.wait("tensor", st_dw.sem, ltok2)
        yfin = AR_BIG[:].rearrange("p (o t) -> p o t", o=16)
        for tb2 in range(2):
            t0 = tb2 * 512
            for st_ in range(4):
                k.wait("tensor", "DV", S_DV.n)
                k.wait("tensor", "AC", S_AC.n)
                for c in range(NCH):
                    mi = T.matmul(PS[4][:, 0:8], arst[:, c, t0 + st_ * 128:t0 + st_ * 128 + 128],
                                  gwT2_sb[:, c, :], start=(c == 0), stop=(c == NCH - 1))
                mm = S_MM.inc(mi)
                k.wait("vector", "MM", mm)
                V.reduce_max(router[:, 64:65], PS[4][:, 0:8], axis=AX.X, negate=True)
                k.wait("scalar", "DV", S_DV.n)
                A.activation(out=router[:, 0:8], in_=PS[4][:, 0:8], func=AF.Exp,
                             bias=router[:, 64:65], scale=1.0, accum_out=router[:, 65:66])
                k.wait("vector", "AC", S_AC.n)
                V.reciprocal(router[:, 66:67], router[:, 65:66])
                V.tensor_scalar_mul(router[:, 0:8], router[:, 0:8], router[:, 66:67])
                V.reduce_max(router[:, 67:68], router[:, 0:8], axis=AX.X)
                V.tensor_scalar(router[:, 16:24], router[:, 0:8], router[:, 67:68], None, op0=ALU.is_ge)
                V.tensor_tensor_scan(router[:, 32:40], router[:, 16:24], router[:, 16:24], zcol[:],
                                     op0=ALU.add, op1=ALU.bypass)
                V.tensor_scalar(router[:, 48:56], router[:, 32:40], 1.0, None, op0=ALU.is_equal)
                V.tensor_tensor(router[:, 16:24], router[:, 16:24], router[:, 48:56], op=ALU.mult)
                V.scalar_tensor_tensor(router[:, 32:40], router[:, 16:24], -2.0, router[:, 0:8],
                                       op0=ALU.mult, op1=ALU.add)
                V.reduce_max(router[:, 68:69], router[:, 32:40], axis=AX.X)
                V.tensor_scalar(router[:, 48:56], router[:, 32:40], router[:, 68:69], None, op0=ALU.is_ge)
                V.tensor_tensor_scan(router[:, 32:40], router[:, 48:56], router[:, 48:56], zcol[:],
                                     op0=ALU.add, op1=ALU.bypass)
                V.tensor_scalar(router[:, 32:40], router[:, 32:40], 1.0, None, op0=ALU.is_equal)
                V.tensor_tensor(router[:, 48:56], router[:, 48:56], router[:, 32:40], op=ALU.mult)
                V.tensor_tensor(router[:, 69:70], router[:, 67:68], router[:, 68:69], op=ALU.add)
                V.reciprocal(router[:, 70:71], router[:, 69:70])
                V.tensor_tensor(router[:, 71:72], router[:, 67:68], router[:, 70:71], op=ALU.mult)
                V.tensor_tensor(router[:, 72:73], router[:, 68:69], router[:, 70:71], op=ALU.mult)
                V.tensor_scalar(router[:, 16:24], router[:, 16:24], router[:, 71:72], None, op0=ALU.mult)
                V.tensor_scalar(router[:, 48:56], router[:, 48:56], router[:, 72:73], None, op0=ALU.mult)
                V.tensor_tensor(westore[:, st_ * 16:st_ * 16 + 8], router[:, 16:24], router[:, 48:56], op=ALU.add)
            dv_we = S_DV.n
            k.wait("tensor", "DV", dv_we)
            for st_ in range(4):
                mi = T.transpose(PS[5][0:8, st_ * 128:(st_ + 1) * 128], westore[:, st_ * 16:st_ * 16 + 8], ident[:])
            mm = S_MM.inc(mi)
            k.wait("vector", "MM", mm)
            V.tensor_copy(stat_tmp[0:8, 512:1024], PS[5][0:8, 0:512])
            dvc = S_DV.n
            k.wait("tensor", "DV", dvc)
            mi = T.matmul(PS[4][:, 0:512], e16_sb[:], stat_tmp[0:8, 512:1024], start=True, stop=True)
            mmc = S_MM.inc(mi)
            for c in range(NCH):
                mi = T.matmul(PS[0][:, :], eA_sb[:, c, :], arst[:, c, t0:t0 + 512], start=(c == 0), stop=(c == NCH - 1))
            mm = S_MM.inc(mi)
            k.wait("vector", "MM", mm)
            V.tensor_copy(stat_tmp[:, 0:512], PS[0][:, :])
            V.tensor_tensor(stat_tmp[:, 0:512], stat_tmp[:, 0:512], PS[4][:, 0:512], op=ALU.mult)
            dvg = S_DV.n
            k.wait("tensor", "DV", dvg)
            ps_e_free = {2: 0, 3: 0}
            for ot in range(16):
                bank = 2 + (ot % 2)
                k.wait("tensor", "DV", ps_e_free[bank])
                mi = T.matmul(PS[bank][:, :], eB_sb[:, ot, :], stat_tmp[:, 0:512], start=True, stop=True)
                mm = S_MM.inc(mi)
                k.wait("vector", "MM", mm)
                V.scalar_tensor_tensor(yfin[:, ot, t0:t0 + 512], PS[bank][:, :], 2.0,
                                       arst[:, ot, t0:t0 + 512], op0=ALU.mult, op1=ALU.add)
                ps_e_free[bank] = S_DV.n
        for tb2 in range(2):
            ln_fm(src_tiles=lambda c, tb2=tb2: yfin[:, c, tb2 * 512:(tb2 + 1) * 512],
                  dst_tiles=lambda c, tb2=tb2: yfin[:, c, tb2 * 512:(tb2 + 1) * 512],
                  psA=PS[0], psB=PS[1], nt=512)
        k.wait("gpsimd", "DV", S_DV.n)
        for c in range(NCH):
            st_y.dma("gpsimd", y[c], yfin[:, c, :])

        k.fence_all()
        k.wait("gpsimd", st_y.sem, st_y.sem.n)
        k.wait("gpsimd", st_tap.sem, st_tap.sem.n)
        k.wait("gpsimd", "CC", S_CC.n)

    mybir.codegen_inst_isa_subclasses(nc)
    return nc


_CACHED = {}


def kernel(**inputs):
    in_maps = prep_inputs(inputs)
    key = DEBUG_TAPS
    if key not in _CACHED:
        _CACHED[key] = build_program()
    nc = _CACHED[key]
    res = run_bass_kernel_spmd(nc, in_maps, core_ids=list(range(NC)))
    out = np.empty((B, S, D), dtype=np.float32)
    for c in range(NC):
        b, g = c // 2, c % 2
        yv = res.results[c]["y"]
        out[b, g * TOK:(g + 1) * TOK] = yv.transpose(2, 0, 1).reshape(TOK, D)
    kernel.last_results = res
    return out
